# version 07
# baseline (speedup 1.0000x reference)
import numpy as np
import ml_dtypes
from contextlib import ExitStack

import concourse.bass as bass
import concourse.mybir as mybir
from concourse.bass_utils import run_bass_kernel_spmd
from concourse.replica_groups import maybe_share_collective_output_space

F32 = mybir.dt.float32
BF16 = mybir.dt.bfloat16
U32 = mybir.dt.uint32
AX = mybir.AxisListType.X
OP = mybir.AluOpType
ACT = mybir.ActivationFunctionType

NC_ = 8
B = 8
SEQ = 4096
D = 1024
NLOC = SEQ // NC_
NT = 32
HO, CO = 16, 64
HI, HDI = 4, 256
E, FF, CAP = 8, 4096, 4096
NEG = -1.0e30


class Ctr:
    def __init__(self, es, nc, name):
        self.h = es.enter_context(nc.semaphore(name))
        self.n = 0


def build():
    nc = bass.Bass("TRN2", target_bir_lowering=False, debug=False,
                   detect_race_conditions=False)

    dt_ = nc.dram_tensor
    x_h = dt_("x", [B, NLOC, D], F32, kind="ExternalInput").ap()
    wq_h = dt_("wq", [D, D], F32, kind="ExternalInput").ap()
    wk_h = dt_("wk", [D, D], F32, kind="ExternalInput").ap()
    wv_h = dt_("wv", [D, D], F32, kind="ExternalInput").ap()
    wo_h = dt_("wo", [D, D], F32, kind="ExternalInput").ap()
    bq_h = dt_("bq", [1, D], F32, kind="ExternalInput").ap()
    bk_h = dt_("bk", [1, D], F32, kind="ExternalInput").ap()
    bv_h = dt_("bv", [1, D], F32, kind="ExternalInput").ap()
    bo_h = dt_("bo", [1, D], F32, kind="ExternalInput").ap()
    ipw_h = dt_("ipw", [3 * D, D], F32, kind="ExternalInput").ap()
    ipb_h = dt_("ipb", [3 * D], F32, kind="ExternalInput").ap()
    opw_h = dt_("opw", [D, D], F32, kind="ExternalInput").ap()
    opb_h = dt_("opb", [1, D], F32, kind="ExternalInput").ap()
    g1_h = dt_("g1", [D], F32, kind="ExternalInput").ap()
    be1_h = dt_("be1", [D], F32, kind="ExternalInput").ap()
    g2_h = dt_("g2", [D], F32, kind="ExternalInput").ap()
    be2_h = dt_("be2", [D], F32, kind="ExternalInput").ap()
    wg_h = dt_("wg", [D, E], F32, kind="ExternalInput").ap()
    w1m_h = dt_("w1m", [D, FF], F32, kind="ExternalInput").ap()
    w2m_h = dt_("w2m", [FF, D], F32, kind="ExternalInput").ap()
    b1m_h = dt_("b1m", [FF], F32, kind="ExternalInput").ap()
    b2m_h = dt_("b2m", [1, D], F32, kind="ExternalInput").ap()
    identb_h = dt_("identb", [128, 128], BF16, kind="ExternalInput").ap()
    identf_h = dt_("identf", [128, 128], F32, kind="ExternalInput").ap()
    maskin_h = dt_("maskin", [128, 128], F32, kind="ExternalInput").ap()
    t128_h = dt_("t128", [128, 128], BF16, kind="ExternalInput").ap()
    bsel_h = dt_("bsel", [128, 8], BF16, kind="ExternalInput").ap()
    bselt_h = dt_("bselt", [8, 128], F32, kind="ExternalInput").ap()
    selc_h = dt_("selc", [64, 8], F32, kind="ExternalInput").ap()
    onesf_h = dt_("onesf", [1, 128], F32, kind="ExternalInput").ap()
    erow_h = dt_("erow", [128, 8], F32, kind="ExternalInput").ap()
    tri16_h = dt_("tri16", [128, 256], BF16, kind="ExternalInput").ap()

    out_h = dt_("out", [B, NLOC, D], F32, kind="ExternalOutput").ap()

    x1sp_h = dt_("x1sp", [NT, 128, D], F32, kind="Internal").ap()
    y1sp_h = dt_("y1sp", [NT, 128, D], F32, kind="Internal").ap()
    RG = [[i for i in range(NC_)]]
    sp_f = maybe_share_collective_output_space("AllGather", RG)
    ccf_in = dt_("ccf_in", [1, 64 + D], F32, kind="Internal").ap()
    ccf_out = dt_("ccf_out", [NC_, 64 + D], F32, kind="Internal",
                  addr_space=sp_f).ap()
    ccw_in = dt_("ccw_in", [128, 8 * D], BF16, kind="Internal").ap()
    ccw_out = dt_("ccw_out", [NC_, 128, 8 * D], BF16, kind="Internal",
                  addr_space=sp_f).ap()

    es = ExitStack()
    sb = lambda name, shape, dtype: es.enter_context(nc.sbuf_tensor(name, shape, dtype))
    ps_ = lambda name, shape, dtype=F32: es.enter_context(nc.psum_tensor(name, shape, dtype))

    arena = sb("arena", [128, 65536], BF16)
    wq_s = arena[:, 0:8192].rearrange("p (c n) -> p c n", c=8)
    wk_s = arena[:, 8192:16384].rearrange("p (c n) -> p c n", c=8)
    wv_s = arena[:, 16384:24576].rearrange("p (c n) -> p c n", c=8)
    wo_s = arena[:, 24576:32768].rearrange("p (c n) -> p c n", c=8)
    ipwt_s = arena[:, 32768:57344].rearrange("p (c n) -> p c n", c=8)
    opwt_s = arena[:, 57344:65536].rearrange("p (c n) -> p c n", c=8)
    w1t_s = arena[:, 0:32768].rearrange("p (c n) -> p c n", c=32)
    w2f_s = arena[:, 32768:65536].rearrange("p (c n) -> p c n", c=32)
    w12_s = arena[:, 0:65536].rearrange("p (e c n) -> p e c n", e=8, c=8)

    g1r = sb("g1r", [128, D], F32)
    be1r = sb("be1r", [128, D], F32)
    wgc = sb("wgc", [128, 8, E], F32)
    identb = sb("identb_s", [128, 128], BF16)
    identf = sb("identf_s", [128, 128], F32)
    maskin = sb("maskin_s", [128, 128], F32)
    t128c = sb("t128c", [128, 128], BF16)
    bselc = sb("bselc", [128, 8], BF16)
    bseltc = sb("bseltc", [8, 128], F32)
    selcc = sb("selcc", [64, 8], F32)
    onesf = sb("onesf_s", [1, 128], F32)
    erow = sb("erow_s", [128, 8], F32)
    tri16_sb = sb("tri16_sb", [128, 256], BF16)
    tri16 = tri16_sb[:].rearrange("p (a b) -> p a b", a=16)
    epsv = sb("epsv", [128, 1], F32)
    brows = sb("brows", [1, 4 * D], F32)
    bcolq = sb("bcolq", [128, 24], F32)
    bcoli = sb("bcoli", [128, 24], F32)
    b1c = sb("b1c", [128, 32], BF16)

    onehot_all = sb("onehot_all", [128, NT, E], BF16)
    prefix_all = sb("prefix_all", [128, NT, E], F32)
    gval_all = sb("gval_all", [128, NT], F32)
    base_all = sb("base_all", [8, NT, E], F32)
    pc_s = sb("pc_s", [8, 8], F32)
    cb_s = sb("cb_s", [8, 8], F32)
    runc = sb("runc", [8, 8], F32)

    xa = sb("xa", [128, 2, D], F32)
    x1t = xa
    y1t = sb("y1t", [128, 2, D], F32)
    xbf = sb("xbf", [128, D], BF16)
    xfm = sb("xfm", [128, 8, 128], BF16)
    qfmst = sb("qfmst", [128, 4, 128], BF16)
    fm32 = sb("fm32", [128, 128], F32)
    qtm = sb("qtm", [128, D], BF16)
    ktm = sb("ktm", [128, D], BF16)
    vtm = sb("vtm", [128, D], BF16)
    sbuf_S = sb("sbuf_S", [128, CO, CO], BF16)
    sbuf_E = sb("sbuf_E", [128, CO, CO], BF16)
    dnum = sb("dnum", [128, CO], F32)
    dden = sb("dden", [128, CO], F32)
    bnd = sb("bnd", [128, 4], F32)
    attn = sb("attn", [128, D], BF16)
    sco = sb("sco", [128, 128], F32)
    ei_s = sb("ei_s", [128, 128], BF16)
    eit_s = sb("eit_s", [128, 128], BF16)
    otm = attn
    stats = sb("stats", [128, 12], F32)
    mv = sb("mv", [128, 2], F32)
    rstd = sb("rstd", [128, 1], F32)
    lg = sb("lg", [128, E], F32)
    eg = sb("eg", [128, E], F32)
    sume = sb("sume", [128, 1], F32)
    gts = sb("gts", [128, E], F32)
    gmx = sb("gmx", [128, 8], F32)
    gmi = sb("gmi", [128, 8], U32)
    idxf = sb("idxf", [128, 1], F32)
    combf = sb("combf", [128, E], F32)
    locs = sb("locs", [128, 1], F32)
    keepv = sb("keepv", [128, 1], F32)
    wall = sb("wall", [128, E], F32)
    wallt = sb("wallt", [8, 128], F32)
    comb8 = sb("comb8", [8, 8], F32)

    qkfm = sbuf_S[:].rearrange("p a b -> p (a b)")[:, 0:2048].rearrange(
        "p (c n) -> p c n", c=16)
    S_f32 = sbuf_S[:].rearrange("p a b -> p (a b)").bitcast(F32)
    E_f32 = sbuf_E[:].rearrange("p a b -> p (a b)").bitcast(F32)
    macc = S_f32[:, 0:1024]
    y2t = E_f32[:, 0:1024]
    b12all = E_f32[0:8, 1024:2048]
    b12mine = S_f32[0:1, 0:1024]
    b2st = S_f32[0:1, 1024:2048]
    w1raw = xbf
    w12st = ktm[:, 0:512]

    pA = ps_("pA", [128, 512])
    pB = ps_("pB", [128, 512])
    pC = ps_("pC", [128, 512])
    pD = ps_("pD", [128, 512])
    pT = ps_("pT", [128, 1024], BF16)
    pTf = ps_("pTf", [128, 512])
    pX = ps_("pX", [128, 512])
    pCNT = ps_("pCNT", [128, 512])

    blkmgr = es.enter_context(nc.Block())

    cT = Ctr(es, nc, "cT")
    cV = Ctr(es, nc, "cV")
    cS = Ctr(es, nc, "cS")
    cG = Ctr(es, nc, "cG")
    cCC = Ctr(es, nc, "cCC")
    sCst = Ctr(es, nc, "sCst")
    sX0 = [Ctr(es, nc, "sX0a"), Ctr(es, nc, "sX0b")]
    sX1 = [Ctr(es, nc, "sX1a"), Ctr(es, nc, "sX1b")]
    sY1 = [Ctr(es, nc, "sY1a"), Ctr(es, nc, "sY1b")]
    sWa = Ctr(es, nc, "sWa")
    sSt = Ctr(es, nc, "sSt")
    sW1 = Ctr(es, nc, "sW1")
    sW2 = Ctr(es, nc, "sW2")
    sCCl = Ctr(es, nc, "sCCl")
    sFst = Ctr(es, nc, "sFst")
    sCin = Ctr(es, nc, "sCin")
    sPB = [Ctr(es, nc, "sPBa"), Ctr(es, nc, "sPBb")]
    sOut = [Ctr(es, nc, "sOuta"), Ctr(es, nc, "sOutb")]
    sB2G = Ctr(es, nc, "sB2G")

    def SY(f):
        @blkmgr.sync
        def _(q):
            f(q)

    def TE(f):
        @blkmgr.tensor
        def _(q):
            f(q)

    def VE(f):
        @blkmgr.vector
        def _(q):
            f(q)

    def SE(f):
        @blkmgr.scalar
        def _(q):
            f(q)

    def GP(f):
        @blkmgr.gpsimd
        def _(q):
            f(q)

    def dma(q, sem, out, in_, slow=False):
        q.dma_start(out=out, in_=in_, allow_slow_non_contiguous=slow
                    ).then_inc(sem.h, 16)
        sem.n += 16

    with es:
        def _pre_sync(q):
            dma(q, sCst, identb[:], identb_h[:])
            dma(q, sCst, identf[:], identf_h[:])
            dma(q, sCst, maskin[:], maskin_h[:])
            dma(q, sCst, t128c[:], t128_h[:])
            dma(q, sCst, bselc[:], bsel_h[:])
            dma(q, sCst, bseltc[:], bselt_h[:])
            dma(q, sCst, selcc[:], selc_h[:])
            dma(q, sCst, onesf[:], onesf_h[:])
            dma(q, sCst, erow[:], erow_h[:])
            dma(q, sCst, tri16_sb[:], tri16_h[:])
            dma(q, sCst, g1r[:], g1_h[None, :].partition_broadcast(128))
            dma(q, sCst, be1r[:], be1_h[None, :].partition_broadcast(128))
            dma(q, sCst, wgc[:], wg_h.rearrange("(c p) e -> p c e", p=128))
            dma(q, sCst, brows[:, 0:D], bo_h[:])
            dma(q, sCst, brows[:, D:2 * D], opb_h[:])
            dma(q, sCst, brows[:, 2 * D:3 * D], bq_h[:])
            dma(q, sCst, brows[:, 3 * D:4 * D], bk_h[:])
            dma(q, sCst, bcolq[:, 0:8], bq_h[0, :].rearrange("(c p) -> p c", p=128), slow=True)
            dma(q, sCst, bcolq[:, 8:16], bk_h[0, :].rearrange("(c p) -> p c", p=128), slow=True)
            dma(q, sCst, bcolq[:, 16:24], bv_h[0, :].rearrange("(c p) -> p c", p=128), slow=True)
            dma(q, sCst, bcoli[:], ipb_h.rearrange("(c p) -> p c", p=128), slow=True)
        SY(_pre_sync)

        def _pre_gp(q):
            for w_s, w_hh in ((wq_s, wq_h), (wk_s, wk_h), (wv_s, wv_h), (wo_s, wo_h)):
                dma(q, sWa, w_s, w_hh.rearrange("(c p) n -> p c n", p=128))
            dma(q, sWa, b1c[:], b1m_h.rearrange("(c p) -> p c", p=128), slow=True)
        GP(_pre_gp)
        pre_w = sWa.n
        pre_cst = sCst.n

        def _pre_v(q):
            q.memset(sbuf_E[:], 0.0)
            q.memset(epsv[:], 1e-5)
            q.memset(base_all[:, 0, :], 0.0)
            q.memset(runc[:], 0.0).then_inc(cV.h, 1); cV.n += 1
        VE(_pre_v)

        for kind in range(2):
            ntile = 24 if kind == 0 else 8
            src_h = ipw_h if kind == 0 else opw_h
            dst_s = ipwt_s if kind == 0 else opwt_s
            for it in range(ntile):
                def _ld(q, it=it, src_h=src_h):
                    q.wait_ge(cS.h, cS.n)
                    dma(q, sSt, w1raw[:], src_h[128 * it:128 * (it + 1), :])
                GP(_ld)
                ld_n = sSt.n

                def _tr(q, ld_n=ld_n):
                    q.wait_ge(sSt.h, ld_n)
                    q.wait_ge(sCst.h, pre_cst)
                    q.wait_ge(cS.h, cS.n)
                    for dc in range(8):
                        q.transpose(pT[:, 128 * dc:128 * dc + 128],
                                    w1raw[:, 128 * dc:128 * dc + 128],
                                    identb[:]).then_inc(cT.h, 1); cT.n += 1
                TE(_tr)
                tr_n = cT.n

                def _ev(q, it=it, tr_n=tr_n, dst_s=dst_s):
                    q.wait_ge(cT.h, tr_n)
                    for dc in range(8):
                        q.copy(out=dst_s[:, dc, 128 * it:128 * (it + 1)],
                               in_=pT[:, 128 * dc:128 * dc + 128]
                               ).then_inc(cS.h, 1); cS.n += 1
                SE(_ev)
        wT_built_cS = cS.n
        wT_built_cT = cT.n

        x0_loaded = [0] * NT
        tile_done_cV = [0] * NT
        x1_stored = [0] * NT
        y1_stored = [0] * NT

        def _ld0(q):
            dma(q, sX0[0], xa[:, 0, :], x_h.transpose([1, 0, 2])[0:16])
        SY(_ld0)
        x0_loaded[0] = sX0[0].n

        for i in range(NT):
            bufi = i % 2
            if i + 1 < NT:
                def _ld(q, i=i):
                    if i >= 1:
                        q.wait_ge(cV.h, tile_done_cV[i - 1])
                        q.wait_ge(sX1[(i - 1) % 2].h, x1_stored[i - 1])
                    dma(q, sX0[(i + 1) % 2], xa[:, (i + 1) % 2, :],
                        x_h.transpose([1, 0, 2])[16 * (i + 1):16 * (i + 2)])
                SY(_ld)
                x0_loaded[i + 1] = sX0[(i + 1) % 2].n

            def _cast(q, i=i, bufi=bufi):
                q.wait_ge(sX0[i % 2].h, x0_loaded[i])
                if i == 0:
                    q.wait_ge(sCst.h, pre_cst)
                    q.wait_ge(sWa.h, pre_w)
                    q.wait_ge(cT.h, wT_built_cT)
                q.tensor_copy(out=xbf[:], in_=xa[:, bufi, :]).then_inc(cV.h, 1); cV.n += 1
            VE(_cast)
            cast_cV = cV.n

            def _tr(q, i=i, cast_cV=cast_cV):
                q.wait_ge(cV.h, cast_cV)
                if i == 0:
                    q.wait_ge(cS.h, wT_built_cS)
                q.wait_ge(cS.h, cS.n)
                for dc in range(8):
                    q.transpose(pT[:, 128 * dc:128 * dc + 128],
                                xbf[:, 128 * dc:128 * dc + 128],
                                identb[:]).then_inc(cT.h, 1); cT.n += 1
            TE(_tr)
            tr_cT = cT.n

            def _trev(q, tr_cT=tr_cT):
                q.wait_ge(cT.h, tr_cT)
                for dc in range(8):
                    q.copy(out=xfm[:, dc, :],
                           in_=pT[:, 128 * dc:128 * dc + 128]).then_inc(cS.h, 1); cS.n += 1
            SE(_trev)
            xfm_cS = cS.n

            def _qkmm_o(q, xfm_cS=xfm_cS):
                q.wait_ge(cS.h, xfm_cS)
                q.wait_ge(cV.h, cV.n)
                banks = (pA, pB, pC, pD)
                for dc in range(8):
                    for bi, (w_s, half) in enumerate(((wq_s, 0), (wq_s, 1),
                                                      (wk_s, 0), (wk_s, 1))):
                        q.matmul(banks[bi][:, 0:512], xfm[:, dc, :],
                                 w_s[:, dc, 512 * half:512 * half + 512],
                                 start=(dc == 0), stop=False)
                for bi, boff in enumerate((2 * D, 2 * D + 512, 3 * D, 3 * D + 512)):
                    mmi = q.matmul(banks[bi][:, 0:512], onesf[0:1, :],
                                   brows[0:1, boff:boff + 512],
                                   start=False, stop=True)
                mmi.then_inc(cT.h, 1); cT.n += 1
            TE(_qkmm_o)
            qko_cT = cT.n

            def _qkev_o(q, qko_cT=qko_cT):
                q.wait_ge(cT.h, qko_cT)
                q.tensor_copy(out=qtm[:, 0:512], in_=pA[:, 0:512]).then_inc(cV.h, 1); cV.n += 1
                q.tensor_copy(out=qtm[:, 512:1024], in_=pB[:, 0:512]).then_inc(cV.h, 1); cV.n += 1
                q.tensor_copy(out=ktm[:, 0:512], in_=pC[:, 0:512]).then_inc(cV.h, 1); cV.n += 1
                q.tensor_copy(out=ktm[:, 512:1024], in_=pD[:, 0:512]).then_inc(cV.h, 1); cV.n += 1
            VE(_qkev_o)
            qkev_cV = cV.n

            for vg in range(2):
                def _vmm_o(q, vg=vg, xfm_cS=xfm_cS, qkev_cV=qkev_cV):
                    q.wait_ge(cS.h, max(xfm_cS, cS.n))
                    q.wait_ge(cV.h, qkev_cV)
                    for k_ in range(4):
                        vc = 4 * vg + k_
                        pp = (pA, pB, pC, pD)[k_]
                        for dc in range(8):
                            mmi = q.matmul(pp[:, 0:128],
                                           wv_s[:, dc, 128 * vc:128 * vc + 128],
                                           xfm[:, dc, :], start=(dc == 0),
                                           stop=(dc == 7))
                    mmi.then_inc(cT.h, 1); cT.n += 1
                TE(_vmm_o)
                vo_cT = cT.n

                def _vev_o(q, vg=vg, vo_cT=vo_cT):
                    q.wait_ge(cT.h, vo_cT)
                    for k_ in range(4):
                        vc = 4 * vg + k_
                        q.activation(out=qfmst[:, k_, :], in_=(pA, pB, pC, pD)[k_][:, 0:128],
                                     func=ACT.Identity, bias=bcolq[:, 16 + vc:16 + vc + 1],
                                     scale=1.0).then_inc(cS.h, 1); cS.n += 1
                SE(_vev_o)
                vev_cS = cS.n

                def _vtr_o(q, vev_cS=vev_cS):
                    q.wait_ge(cS.h, vev_cS)
                    q.wait_ge(cV.h, cV.n)
                    for k_ in range(4):
                        q.transpose(pT[:, 128 * k_:128 * k_ + 128], qfmst[:, k_, :],
                                    identb[:]).then_inc(cT.h, 1); cT.n += 1
                TE(_vtr_o)
                vtr_cT = cT.n

                def _vtrev_o(q, vg=vg, vtr_cT=vtr_cT):
                    q.wait_ge(cT.h, vtr_cT)
                    for k_ in range(4):
                        vc = 4 * vg + k_
                        q.tensor_copy(out=vtm[:, 128 * vc:128 * vc + 128],
                                      in_=pT[:, 128 * k_:128 * k_ + 128]
                                      ).then_inc(cV.h, 1); cV.n += 1
                VE(_vtrev_o)
            qkv_cV = cV.n

            def _bnd(q, qkv_cV=qkv_cV):
                q.wait_ge(cV.h, qkv_cV)
                q.reduce_max(bnd[:, 0:1], qtm[:, :], axis=AX,
                             apply_absolute_value=True).then_inc(cV.h, 1); cV.n += 1
                q.reduce_max(bnd[:, 1:2], ktm[:, :], axis=AX,
                             apply_absolute_value=True).then_inc(cV.h, 1); cV.n += 1
                q.wait_ge(cV.h, cV.n)
                q.tensor_tensor(out=bnd[:, 2:3], in0=bnd[:, 0:1], in1=bnd[:, 1:2],
                                op=OP.mult).then_inc(cV.h, 1); cV.n += 1
                q.wait_ge(cV.h, cV.n)
                q.tensor_scalar_mul(bnd[:, 3:4], bnd[:, 2:3], -0.125
                                    ).then_inc(cV.h, 1); cV.n += 1
            VE(_bnd)
            bnd_cV = cV.n

            QB = [(0, 16), (16, 32), (32, 48), (48, 64)]
            smul_cV = [0] * HO
            sexp_cS = [0] * HO
            gmask_cG = [0] * HO
            den_cV = [0] * HO
            ev_cG = [0] * HO

            def _emit_den(q, h):
                q.wait_ge(cG.h, gmask_cG[h])
                for j, (qa, qb) in enumerate(QB):
                    kk = 16 * (j + 1)
                    q.tensor_reduce(out=dden[:, qa:qb], in_=sbuf_E[:, qa:qb, 0:kk],
                                    axis=AX, op=OP.add).then_inc(cV.h, 1); cV.n += 1

            def _emit_finish(q, h):
                q.wait_ge(cG.h, ev_cG[h])
                for j, (qa, qb) in enumerate(QB):
                    kk = 16 * (j + 1)
                    q.tensor_reduce(out=dnum[:, qa:qb], in_=sbuf_E[:, qa:qb, 0:kk],
                                    axis=AX, op=OP.add).then_inc(cV.h, 1); cV.n += 1
                q.wait_ge(cV.h, cV.n)
                q.reciprocal(dden[:], dden[:]).then_inc(cV.h, 1); cV.n += 1
                q.wait_ge(cV.h, cV.n)
                q.tensor_tensor(out=attn[:, 64 * h:64 * h + 64], in0=dnum[:],
                                in1=dden[:], op=OP.mult).then_inc(cV.h, 1); cV.n += 1

            for h in range(HO):
                if h >= 1:
                    def _vden(q, h=h):
                        _emit_den(q, h - 1)
                    VE(_vden)
                    den_cV[h - 1] = cV.n

                    def _gev(q, h=h):
                        q.wait_ge(cV.h, den_cV[h - 1])
                        for j, (qa, qb) in enumerate(QB):
                            kk = 16 * (j + 1)
                            q.tensor_tensor(
                                out=sbuf_E[:, qa:qb, 0:kk],
                                in0=sbuf_E[:, qa:qb, 0:kk],
                                in1=vtm[:, 64 * (h - 1):64 * (h - 1) + 64][:, None, 0:kk]
                                    .broadcast_to([128, 16, kk]),
                                op=OP.mult).then_inc(cG.h, 1); cG.n += 1
                    GP(_gev)
                    ev_cG[h - 1] = cG.n

                def _vsm(q, h=h):
                    if h == 0:
                        q.wait_ge(cV.h, bnd_cV)
                    else:
                        q.wait_ge(cS.h, sexp_cS[h - 1])
                    for j, (qa, qb) in enumerate(QB):
                        kk = 16 * (j + 1)
                        q.tensor_tensor(
                            out=sbuf_S[:, qa:qb, 0:kk],
                            in0=qtm[:, 64 * h:64 * h + 64][:, qa:qb, None]
                                .broadcast_to([128, 16, kk]),
                            in1=ktm[:, 64 * h:64 * h + 64][:, None, 0:kk]
                                .broadcast_to([128, 16, kk]),
                            op=OP.mult).then_inc(cV.h, 1); cV.n += 1
                    if h >= 1:
                        _emit_finish(q, h - 1)
                VE(_vsm)
                smul_cV[h] = cV.n

                def _sexp(q, h=h):
                    q.wait_ge(cV.h, smul_cV[h])
                    for j, (qa, qb) in enumerate(QB):
                        kk = 16 * (j + 1)
                        q.activation(out=sbuf_E[:, qa:qb, 0:kk],
                                     in_=sbuf_S[:, qa:qb, 0:kk],
                                     func=ACT.Exp, bias=bnd[:, 3:4],
                                     scale=0.125).then_inc(cS.h, 1); cS.n += 1
                SE(_sexp)
                sexp_cS[h] = cS.n

                def _gmask(q, h=h):
                    q.wait_ge(cS.h, sexp_cS[h])
                    for j in range(4):
                        q.tensor_tensor(
                            out=sbuf_E[:, 16 * j:16 * j + 16, 16 * j:16 * j + 16],
                            in0=sbuf_E[:, 16 * j:16 * j + 16, 16 * j:16 * j + 16],
                            in1=tri16[:, :, :], op=OP.mult).then_inc(cG.h, 1); cG.n += 1
                GP(_gmask)
                gmask_cG[h] = cG.n

            def _vdrain(q):
                _emit_den(q, HO - 1)
            VE(_vdrain)
            den_cV[HO - 1] = cV.n

            def _gdrain(q):
                q.wait_ge(cV.h, den_cV[HO - 1])
                for j, (qa, qb) in enumerate(QB):
                    kk = 16 * (j + 1)
                    q.tensor_tensor(
                        out=sbuf_E[:, qa:qb, 0:kk],
                        in0=sbuf_E[:, qa:qb, 0:kk],
                        in1=vtm[:, 64 * (HO - 1):64 * (HO - 1) + 64][:, None, 0:kk]
                            .broadcast_to([128, 16, kk]),
                        op=OP.mult).then_inc(cG.h, 1); cG.n += 1
            GP(_gdrain)
            ev_cG[HO - 1] = cG.n

            def _vfin(q):
                _emit_finish(q, HO - 1)
            VE(_vfin)
            attn_cV = cV.n

            def _atr(q, attn_cV=attn_cV):
                q.wait_ge(cV.h, attn_cV)
                q.wait_ge(cS.h, cS.n)
                for dc in range(8):
                    q.transpose(pT[:, 128 * dc:128 * dc + 128],
                                attn[:, 128 * dc:128 * dc + 128],
                                identb[:]).then_inc(cT.h, 1); cT.n += 1
            TE(_atr)
            atr_cT = cT.n

            def _atrev(q, atr_cT=atr_cT):
                q.wait_ge(cT.h, atr_cT)
                for dc in range(8):
                    q.copy(out=xfm[:, dc, :],
                           in_=pT[:, 128 * dc:128 * dc + 128]).then_inc(cS.h, 1); cS.n += 1
            SE(_atrev)
            afm_cS = cS.n

            def _womm(q, afm_cS=afm_cS):
                q.wait_ge(cS.h, afm_cS)
                q.wait_ge(cV.h, cV.n)
                for half, pp in ((0, pA), (1, pB)):
                    for dc in range(8):
                        q.matmul(pp[:, 0:512], xfm[:, dc, :],
                                 wo_s[:, dc, 512 * half:512 * half + 512],
                                 start=(dc == 0), stop=False)
                    q.matmul(pp[:, 0:512], onesf[0:1, :],
                             brows[0:1, 512 * half:512 * half + 512],
                             start=False, stop=True).then_inc(cT.h, 1); cT.n += 1
            TE(_womm)
            wo_cT = cT.n

            def _x1pre(q, bufi=bufi, wo_cT=wo_cT, i=i):
                q.wait_ge(cT.h, wo_cT)
                if i >= 2:
                    q.wait_ge(sX1[i % 2].h, x1_stored[i - 2])
                q.tensor_tensor(out=xa[:, bufi, 0:512], in0=xa[:, bufi, 0:512],
                                in1=pA[:, 0:512], op=OP.add).then_inc(cV.h, 1); cV.n += 1
                q.tensor_tensor(out=xa[:, bufi, 512:1024], in0=xa[:, bufi, 512:1024],
                                in1=pB[:, 0:512], op=OP.add).then_inc(cV.h, 1); cV.n += 1
                q.wait_ge(cV.h, cV.n)
                q.bn_stats(stats[:, 0:6], x1t[:, bufi, 0:512]).then_inc(cV.h, 1); cV.n += 1
                q.bn_stats(stats[:, 6:12], x1t[:, bufi, 512:1024]).then_inc(cV.h, 1); cV.n += 1
                q.wait_ge(cV.h, cV.n)
                q.bn_aggr(mv[:], stats[:]).then_inc(cV.h, 1); cV.n += 1
            VE(_x1pre)
            x1pre_cV = cV.n

            def _rstd0(q, x1pre_cV=x1pre_cV):
                q.wait_ge(cV.h, x1pre_cV)
                q.activation(out=rstd[:], in_=mv[:, 1:2], func=ACT.Ln,
                             bias=epsv[:, 0:1], scale=1.0).then_inc(cS.h, 1); cS.n += 1
                q.wait_ge(cS.h, cS.n)
                q.activation(out=rstd[:], in_=rstd[:], func=ACT.Exp,
                             scale=-0.5).then_inc(cS.h, 1); cS.n += 1
            SE(_rstd0)
            rstd0_cS = cS.n

            def _ln0(q, bufi=bufi, rstd0_cS=rstd0_cS):
                q.wait_ge(cS.h, rstd0_cS)
                q.tensor_scalar(out=x1t[:, bufi, :], in0=x1t[:, bufi, :],
                                scalar1=mv[:, 0:1], scalar2=rstd[:, 0:1],
                                op0=OP.subtract, op1=OP.mult).then_inc(cV.h, 1); cV.n += 1
            VE(_ln0)
            x1_cV = cV.n

            def _stx1(q, i=i, bufi=bufi, x1_cV=x1_cV):
                q.wait_ge(cV.h, x1_cV)
                dma(q, sX1[i % 2], x1sp_h[i], x1t[:, bufi, :])
            SY(_stx1)
            x1_stored[i] = sX1[i % 2].n

            def _cast2(q, bufi=bufi, x1_cV=x1_cV):
                q.wait_ge(cV.h, x1_cV)
                q.tensor_copy(out=xbf[:], in_=x1t[:, bufi, :]).then_inc(cV.h, 1); cV.n += 1
            VE(_cast2)
            cast2_cV = cV.n

            def _tr2(q, cast2_cV=cast2_cV):
                q.wait_ge(cV.h, cast2_cV)
                q.wait_ge(cS.h, cS.n)
                for dc in range(8):
                    q.transpose(pT[:, 128 * dc:128 * dc + 128],
                                xbf[:, 128 * dc:128 * dc + 128],
                                identb[:]).then_inc(cT.h, 1); cT.n += 1
            TE(_tr2)
            tr2_cT = cT.n

            def _tr2ev(q, tr2_cT=tr2_cT):
                q.wait_ge(cT.h, tr2_cT)
                for dc in range(8):
                    q.copy(out=xfm[:, dc, :],
                           in_=pT[:, 128 * dc:128 * dc + 128]).then_inc(cS.h, 1); cS.n += 1
            SE(_tr2ev)
            x1fm_cS = cS.n

            for icg in range(4):
                def _qkmm(q, icg=icg, x1fm_cS=x1fm_cS):
                    q.wait_ge(cS.h, x1fm_cS)
                    q.wait_ge(cS.h, cS.n)
                    for k_ in range(4):
                        ic = 4 * icg + k_
                        pp = (pA, pB, pC, pD)[k_]
                        for dc in range(8):
                            mmi = q.matmul(pp[:, 0:128],
                                           ipwt_s[:, dc, 128 * ic:128 * ic + 128],
                                           xfm[:, dc, :], start=(dc == 0),
                                           stop=(dc == 7))
                    mmi.then_inc(cT.h, 1); cT.n += 1
                TE(_qkmm)
                qk_cT = cT.n

                def _qkev(q, icg=icg, qk_cT=qk_cT):
                    q.wait_ge(cT.h, qk_cT)
                    for k_ in range(4):
                        ic = 4 * icg + k_
                        pp = (pA, pB, pC, pD)[k_]
                        q.activation(out=qkfm[:, ic, :], in_=pp[:, 0:128],
                                     func=ACT.Identity, bias=bcoli[:, ic:ic + 1],
                                     scale=1.0).then_inc(cS.h, 1); cS.n += 1
                SE(_qkev)
            qkfm_cS = cS.n

            for vg in range(2):
                def _vmm(q, vg=vg, x1fm_cS=x1fm_cS):
                    q.wait_ge(cS.h, x1fm_cS)
                    q.wait_ge(cS.h, cS.n)
                    for k_ in range(4):
                        vc = 4 * vg + k_
                        pp = (pA, pB, pC, pD)[k_]
                        for dc in range(8):
                            mmi = q.matmul(pp[:, 0:128],
                                           ipwt_s[:, dc, 2048 + 128 * vc:2048 + 128 * vc + 128],
                                           xfm[:, dc, :], start=(dc == 0),
                                           stop=(dc == 7))
                    mmi.then_inc(cT.h, 1); cT.n += 1
                TE(_vmm)
                vmm_cT = cT.n

                def _vev(q, vg=vg, vmm_cT=vmm_cT):
                    q.wait_ge(cT.h, vmm_cT)
                    for k_ in range(4):
                        vc = 4 * vg + k_
                        q.activation(out=qfmst[:, k_, :], in_=(pA, pB, pC, pD)[k_][:, 0:128],
                                     func=ACT.Identity, bias=bcoli[:, 16 + vc:16 + vc + 1],
                                     scale=1.0).then_inc(cS.h, 1); cS.n += 1
                SE(_vev)
                vev_cS = cS.n

                def _vtr(q, vev_cS=vev_cS):
                    q.wait_ge(cS.h, vev_cS)
                    q.wait_ge(cV.h, cV.n)
                    for k_ in range(4):
                        q.transpose(pT[:, 128 * k_:128 * k_ + 128], qfmst[:, k_, :],
                                    identb[:]).then_inc(cT.h, 1); cT.n += 1
                TE(_vtr)
                vtr_cT = cT.n

                def _vtrev(q, vg=vg, vtr_cT=vtr_cT):
                    q.wait_ge(cT.h, vtr_cT)
                    for k_ in range(4):
                        vc = 4 * vg + k_
                        q.tensor_copy(out=vtm[:, 128 * vc:128 * vc + 128],
                                      in_=pT[:, 128 * k_:128 * k_ + 128]
                                      ).then_inc(cV.h, 1); cV.n += 1
                VE(_vtrev)
            vtm_cV = cV.n

            for h in range(HI):
                def _scmm(q, h=h, qkfm_cS=qkfm_cS):
                    q.wait_ge(cS.h, qkfm_cS)
                    q.wait_ge(cV.h, cV.n)
                    q.matmul(pTf[:, 0:128], qkfm[:, 2 * h, :], qkfm[:, 8 + 2 * h, :],
                             start=True, stop=False)
                    q.matmul(pTf[:, 0:128], qkfm[:, 2 * h + 1, :],
                             qkfm[:, 8 + 2 * h + 1, :], start=False,
                             stop=True).then_inc(cT.h, 1); cT.n += 1
                TE(_scmm)
                sc_cT = cT.n

                def _scsm(q, sc_cT=sc_cT):
                    q.wait_ge(cT.h, sc_cT)
                    q.tensor_tensor(out=sco[:], in0=pTf[:, 0:128], in1=maskin[:],
                                    op=OP.add).then_inc(cV.h, 1); cV.n += 1
                    q.wait_ge(cV.h, cV.n)
                    q.reduce_max(bnd[:, 0:1], sco[:], axis=AX,
                                 negate=True).then_inc(cV.h, 1); cV.n += 1
                    q.wait_ge(cV.h, cV.n)
                    q.tensor_scalar_mul(bnd[:, 1:2], bnd[:, 0:1], 0.0625
                                        ).then_inc(cV.h, 1); cV.n += 1
                VE(_scsm)
                scs_cV = cV.n

                def _sexp2(q, scs_cV=scs_cV):
                    q.wait_ge(cV.h, scs_cV)
                    q.activation(out=ei_s[:], in_=sco[:], func=ACT.Exp,
                                 bias=bnd[:, 1:2], scale=0.0625,
                                 accum_out=bnd[:, 2:3]).then_inc(cS.h, 1); cS.n += 1
                SE(_sexp2)
                se2_cS = cS.n

                def _eitr(q, se2_cS=se2_cS):
                    q.wait_ge(cS.h, se2_cS)
                    q.wait_ge(cV.h, cV.n)
                    q.transpose(pT[:, 0:128], ei_s[:], identb[:]).then_inc(cT.h, 1)
                    cT.n += 1
                TE(_eitr)
                eit_cT = cT.n

                def _eitev(q, eit_cT=eit_cT):
                    q.wait_ge(cT.h, eit_cT)
                    q.reciprocal(bnd[:, 3:4], bnd[:, 2:3]).then_inc(cV.h, 1); cV.n += 1
                    q.tensor_copy(out=eit_s[:], in_=pT[:, 0:128]).then_inc(cV.h, 1)
                    cV.n += 1
                VE(_eitev)
                eit_cV = cV.n

                def _avmm(q, h=h, eit_cV=eit_cV, vtm_cV=vtm_cV):
                    q.wait_ge(cV.h, max(eit_cV, vtm_cV))
                    q.matmul(pTf[:, 128:384], eit_s[:], vtm[:, 256 * h:256 * h + 256],
                             start=True, stop=True).then_inc(cT.h, 1); cT.n += 1
                TE(_avmm)
                av_cT = cT.n

                def _avev(q, h=h, av_cT=av_cT):
                    q.wait_ge(cT.h, av_cT)
                    q.tensor_scalar(out=otm[:, 256 * h:256 * h + 256],
                                    in0=pTf[:, 128:384], scalar1=bnd[:, 3:4],
                                    scalar2=None, op0=OP.mult).then_inc(cV.h, 1)
                    cV.n += 1
                VE(_avev)
            otm_cV = cV.n

            def _otr(q, otm_cV=otm_cV):
                q.wait_ge(cV.h, otm_cV)
                q.wait_ge(cS.h, cS.n)
                for dc in range(8):
                    q.transpose(pT[:, 128 * dc:128 * dc + 128],
                                otm[:, 128 * dc:128 * dc + 128],
                                identb[:]).then_inc(cT.h, 1); cT.n += 1
            TE(_otr)
            otr_cT = cT.n

            def _otrev(q, otr_cT=otr_cT):
                q.wait_ge(cT.h, otr_cT)
                for dc in range(8):
                    q.copy(out=xfm[:, dc, :],
                           in_=pT[:, 128 * dc:128 * dc + 128]).then_inc(cS.h, 1); cS.n += 1
            SE(_otrev)
            ofm_cS = cS.n

            def _opmm(q, ofm_cS=ofm_cS):
                q.wait_ge(cS.h, ofm_cS)
                q.wait_ge(cV.h, cV.n)
                for half, pp in ((0, pC), (1, pD)):
                    for dc in range(8):
                        q.matmul(pp[:, 0:512], xfm[:, dc, :],
                                 opwt_s[:, dc, 512 * half:512 * half + 512],
                                 start=(dc == 0), stop=False)
                    q.matmul(pp[:, 0:512], onesf[0:1, :],
                             brows[0:1, D + 512 * half:D + 512 * half + 512],
                             start=False, stop=True).then_inc(cT.h, 1); cT.n += 1
            TE(_opmm)
            op_cT = cT.n

            def _y1pre(q, bufi=bufi, op_cT=op_cT, i=i):
                q.wait_ge(cT.h, op_cT)
                if i >= 2:
                    q.wait_ge(sY1[i % 2].h, y1_stored[i - 2])
                q.tensor_tensor(out=y1t[:, bufi, 0:512], in0=x1t[:, bufi, 0:512],
                                in1=pC[:, 0:512], op=OP.add).then_inc(cV.h, 1); cV.n += 1
                q.tensor_tensor(out=y1t[:, bufi, 512:1024], in0=x1t[:, bufi, 512:1024],
                                in1=pD[:, 0:512], op=OP.add).then_inc(cV.h, 1); cV.n += 1
                q.wait_ge(cV.h, cV.n)
                q.bn_stats(stats[:, 0:6], y1t[:, bufi, 0:512]).then_inc(cV.h, 1); cV.n += 1
                q.bn_stats(stats[:, 6:12], y1t[:, bufi, 512:1024]).then_inc(cV.h, 1); cV.n += 1
                q.wait_ge(cV.h, cV.n)
                q.bn_aggr(mv[:], stats[:]).then_inc(cV.h, 1); cV.n += 1
            VE(_y1pre)
            y1pre_cV = cV.n

            def _rstd1(q, y1pre_cV=y1pre_cV):
                q.wait_ge(cV.h, y1pre_cV)
                q.activation(out=rstd[:], in_=mv[:, 1:2], func=ACT.Ln,
                             bias=epsv[:, 0:1], scale=1.0).then_inc(cS.h, 1); cS.n += 1
                q.wait_ge(cS.h, cS.n)
                q.activation(out=rstd[:], in_=rstd[:], func=ACT.Exp,
                             scale=-0.5).then_inc(cS.h, 1); cS.n += 1
            SE(_rstd1)
            rstd1_cS = cS.n

            def _ln1(q, bufi=bufi, rstd1_cS=rstd1_cS):
                q.wait_ge(cS.h, rstd1_cS)
                q.tensor_scalar(out=y1t[:, bufi, :], in0=y1t[:, bufi, :],
                                scalar1=mv[:, 0:1], scalar2=rstd[:, 0:1],
                                op0=OP.subtract, op1=OP.mult).then_inc(cV.h, 1); cV.n += 1
                q.wait_ge(cV.h, cV.n)
                q.tensor_mul(y1t[:, bufi, :], y1t[:, bufi, :], g1r[:]
                             ).then_inc(cV.h, 1); cV.n += 1
                q.wait_ge(cV.h, cV.n)
                q.tensor_add(y1t[:, bufi, :], y1t[:, bufi, :], be1r[:]
                             ).then_inc(cV.h, 1); cV.n += 1
            VE(_ln1)
            y1_cV = cV.n

            def _sty1(q, i=i, bufi=bufi, y1_cV=y1_cV):
                q.wait_ge(cV.h, y1_cV)
                dma(q, sY1[i % 2], y1sp_h[i], y1t[:, bufi, :])
            SY(_sty1)
            y1_stored[i] = sY1[i % 2].n

            for dc in range(8):
                slot = 128 * (dc % 2)

                def _t(q, dc=dc, slot=slot, y1_cV=y1_cV, bufi=bufi):
                    q.wait_ge(cV.h, max(y1_cV, cV.n))
                    q.transpose(pTf[:, slot:slot + 128],
                                y1t[:, bufi, 128 * dc:128 * dc + 128],
                                identf[:]).then_inc(cT.h, 1); cT.n += 1
                TE(_t)
                t_n = cT.n

                def _e(q, slot=slot, t_n=t_n):
                    q.wait_ge(cT.h, t_n)
                    q.tensor_copy(out=fm32[:], in_=pTf[:, slot:slot + 128]
                                  ).then_inc(cV.h, 1); cV.n += 1
                VE(_e)
                e_n = cV.n

                def _lgmm(q, dc=dc, e_n=e_n, i=i):
                    q.wait_ge(cV.h, e_n)
                    q.matmul(pX[:, 0:8], fm32[:], wgc[:, dc, :],
                             start=(dc == 0), stop=(dc == 7)).then_inc(cT.h, 1); cT.n += 1
                TE(_lgmm)
            lg_cT = cT.n

            def _gates(q, lg_cT=lg_cT):
                q.wait_ge(cT.h, lg_cT)
                q.tensor_copy(out=lg[:], in_=pX[:, 0:8]).then_inc(cV.h, 1); cV.n += 1
                q.wait_ge(cV.h, cV.n)
                q.reduce_max(bnd[:, 0:1], lg[:], axis=AX, negate=True
                             ).then_inc(cV.h, 1); cV.n += 1
            VE(_gates)
            gmax_cV = cV.n

            def _gexp(q, gmax_cV=gmax_cV):
                q.wait_ge(cV.h, gmax_cV)
                q.activation(out=eg[:], in_=lg[:], func=ACT.Exp, bias=bnd[:, 0:1],
                             scale=1.0, accum_out=sume[:, 0:1]).then_inc(cS.h, 1)
                cS.n += 1
            SE(_gexp)
            ge_cS = cS.n

            def _gsel(q, i=i, ge_cS=ge_cS):
                q.wait_ge(cS.h, ge_cS)
                q.reciprocal(sume[:], sume[:]).then_inc(cV.h, 1); cV.n += 1
                q.wait_ge(cV.h, cV.n)
                q.tensor_scalar(out=gts[:], in0=eg[:], scalar1=sume[:, 0:1],
                                scalar2=None, op0=OP.mult).then_inc(cV.h, 1); cV.n += 1
                q.wait_ge(cV.h, cV.n)
                q.max(gmx[:], gts[:]).then_inc(cV.h, 1); cV.n += 1
                q.wait_ge(cV.h, cV.n)
                q.max_index(gmi[:], gmx[:], gts[:]).then_inc(cV.h, 1); cV.n += 1
                q.wait_ge(cV.h, cV.n)
                q.tensor_copy(out=gval_all[:, i:i + 1], in_=gmx[:, 0:1])
                q.tensor_copy(out=idxf[:], in_=gmi[:, 0:1]).then_inc(cV.h, 1); cV.n += 1
                q.wait_ge(cV.h, cV.n)
                q.tensor_tensor(out=gts[:], in0=idxf[:].broadcast_to([128, E]),
                                in1=erow[:], op=OP.is_equal).then_inc(cV.h, 1); cV.n += 1
                q.wait_ge(cV.h, cV.n)
                q.tensor_copy(out=onehot_all[:, i, :], in_=gts[:]).then_inc(cV.h, 1)
                cV.n += 1
            VE(_gsel)
            oh_cV = cV.n

            if i > 0:
                def _basev(q, i=i):
                    q.wait_ge(cV.h, cV.n)
                    q.tensor_copy(out=base_all[:, i, :], in_=runc[:]
                                  ).then_inc(cV.h, 1); cV.n += 1
                VE(_basev)
            bv_cV = cV.n

            def _cnt(q, i=i, oh_cV=oh_cV, bv_cV=bv_cV):
                q.wait_ge(cV.h, max(oh_cV, bv_cV))
                q.matmul(pX[:, 8:16], t128c[:], onehot_all[:, i, :], start=True,
                         stop=True)
                q.matmul(pCNT[0:8, 0:8], bselc[:], onehot_all[:, i, :],
                         start=True, stop=True).then_inc(cT.h, 1)
                cT.n += 1
            TE(_cnt)
            cnt_cT = cT.n

            def _pfev(q, i=i, cnt_cT=cnt_cT):
                q.wait_ge(cT.h, cnt_cT)
                q.wait_ge(cV.h, cV.n)
                q.tensor_copy(out=prefix_all[:, i, :], in_=pX[:, 8:16]
                              ).then_inc(cV.h, 1); cV.n += 1
                q.tensor_tensor(out=runc[:], in0=runc[:], in1=pCNT[0:8, 0:8],
                                op=OP.add).then_inc(cV.h, 1); cV.n += 1
            VE(_pfev)

            tile_done_cV[i] = cV.n

        pa_done_cT = cT.n

        def _pcev(q):
            q.wait_ge(cV.h, cV.n)
            q.tensor_copy(out=pc_s[:], in_=runc[:]).then_inc(cV.h, 1); cV.n += 1
        VE(_pcev)
        pc_cV = cV.n

        def _ldb2(q):
            q.wait_ge(cV.h, pc_cV)
            dma(q, sB2G, b2st, b2m_h[:])
        SY(_ldb2)
        b2_n = sB2G.n

        w1t_ld = [0] * 32
        for fc in range(32):
            def _ldw1(q, fc=fc):
                if fc == 0:
                    q.wait_ge(cT.h, pa_done_cT)
                    q.wait_ge(cV.h, pc_cV)
                q.wait_ge(cS.h, cS.n)
                dma(q, sW1, w1raw[:],
                    w1m_h[:, 128 * fc:128 * fc + 128].rearrange("(c p) f -> p c f", p=128))
            GP(_ldw1)
            ld_n = sW1.n

            def _trw1(q, ld_n=ld_n):
                q.wait_ge(sW1.h, ld_n)
                q.wait_ge(cS.h, cS.n)
                for dc in range(8):
                    q.transpose(pT[:, 128 * dc:128 * dc + 128],
                                w1raw[:, 128 * dc:128 * dc + 128],
                                identb[:]).then_inc(cT.h, 1); cT.n += 1
            TE(_trw1)
            tr_n = cT.n

            def _evw1(q, fc=fc, tr_n=tr_n):
                q.wait_ge(cT.h, tr_n)
                for dc in range(8):
                    q.copy(out=w1t_s[:, fc, 128 * dc:128 * dc + 128],
                           in_=pT[:, 128 * dc:128 * dc + 128]).then_inc(cS.h, 1)
                    cS.n += 1
            SE(_evw1)
            w1t_ld[fc] = cS.n

        def _ldw2(q):
            q.wait_ge(cT.h, pa_done_cT)
            for fc in range(32):
                dma(q, sW2, w2f_s[:, fc, :], w2m_h[128 * fc:128 * fc + 128, :])
        GP(_ldw2)
        w2_n = sW2.n

        for half in range(2):
            def _mm(q, half=half):
                q.wait_ge(sW2.h, w2_n)
                q.wait_ge(cS.h, w1t_ld[31])
                q.wait_ge(cV.h, pc_cV if half == 0 else cV.n)
                for fc in range(32):
                    mmi = q.matmul(pX[0:1, 0:512], b1c[:, fc:fc + 1],
                                   w2f_s[:, fc, 512 * half:512 * half + 512],
                                   start=(fc == 0), stop=(fc == 31))
                mmi.then_inc(cT.h, 1); cT.n += 1
            TE(_mm)
            mm_n = cT.n

            def _ev(q, half=half, mm_n=mm_n):
                q.wait_ge(cT.h, mm_n)
                q.wait_ge(sB2G.h, b2_n)
                q.tensor_tensor(out=b12mine[0:1, 512 * half:512 * half + 512],
                                in0=pX[0:1, 0:512],
                                in1=b2st[0:1, 512 * half:512 * half + 512],
                                op=OP.add).then_inc(cV.h, 1); cV.n += 1
            VE(_ev)
        b12_cV = cV.n

        for dc in range(8):
            for half in range(2):
                pp = (pA, pB, pC, pD)[(2 * dc + half) % 4]

                def _fmm(q, dc=dc, half=half, pp=pp):
                    q.wait_ge(cS.h, w1t_ld[31])
                    q.wait_ge(sW2.h, w2_n)
                    q.wait_ge(cV.h, cV.n)
                    for fc in range(32):
                        mmi = q.matmul(pp[:, 0:512], w1t_s[:, fc, 128 * dc:128 * dc + 128],
                                       w2f_s[:, fc, 512 * half:512 * half + 512],
                                       start=(fc == 0), stop=(fc == 31))
                    mmi.then_inc(cT.h, 1); cT.n += 1
                TE(_fmm)
                f_n = cT.n

                def _fev(q, pp=pp, f_n=f_n):
                    q.wait_ge(cT.h, f_n)
                    q.wait_ge(sFst.h, sFst.n)
                    q.tensor_copy(out=w12st, in_=pp[:, 0:512]).then_inc(cV.h, 1)
                    cV.n += 1
                VE(_fev)
                e_n = cV.n

                def _fst(q, dc=dc, half=half, e_n=e_n):
                    q.wait_ge(cV.h, e_n)
                    dma(q, sFst, ccw_in[:, 1024 * dc + 512 * half:1024 * dc + 512 * half + 512],
                        w12st)
                SY(_fst)
        fold_n = sFst.n

        def _ccin(q):
            q.wait_ge(cV.h, max(pc_cV, b12_cV))
            dma(q, sCin, ccf_in[0:1, 0:64].rearrange("a (b e) -> (a b) e", e=8), pc_s[:])
            dma(q, sCin, ccf_in[0:1, 64:64 + D], b12mine)
        SY(_ccin)
        ccin_n = sCin.n

        def _cc(q):
            q.wait_ge(sFst.h, fold_n)
            q.wait_ge(sCin.h, ccin_n)
            q.collective_compute("AllGather", OP.bypass, ins=[ccf_in[:]],
                                 outs=[ccf_out[:]],
                                 replica_groups=RG).then_inc(cCC.h, 1); cCC.n += 1
            q.collective_compute("AllGather", OP.bypass, ins=[ccw_in[:]],
                                 outs=[ccw_out[:]],
                                 replica_groups=RG).then_inc(cCC.h, 1); cCC.n += 1
            q.wait_ge(cCC.h, cCC.n)
            dma(q, sCCl, w12_s.rearrange("p e c n -> p (e c n)"), ccw_out.transpose([1, 0, 2]))
            dma(q, sCCl, b12all, ccf_out[:, 64:64 + D])
            dma(q, sCCl, sco[0:64, 0:8], ccf_out[:, 0:64].rearrange("c (b e) -> c b e", e=8))
        GP(_cc)
        ccout_n = sCCl.n

        def _ld_g2(q):
            q.wait_ge(cV.h, cV.n)
            dma(q, sB2G, g1r[:], g2_h[None, :].partition_broadcast(128))
            dma(q, sB2G, be1r[:], be2_h[None, :].partition_broadcast(128))
        SY(_ld_g2)
        g2_n = sB2G.n

        def _cbase(q):
            q.wait_ge(sCCl.h, ccout_n)
            q.matmul(pX[0:8, 0:8], selcc[:], sco[0:64, 0:8], start=True,
                     stop=True).then_inc(cT.h, 1); cT.n += 1
        TE(_cbase)
        cb_cT = cT.n

        def _cbev(q):
            q.wait_ge(cT.h, cb_cT)
            q.tensor_copy(out=cb_s[:], in_=pX[0:8, 0:8]).then_inc(cV.h, 1); cV.n += 1
        VE(_cbev)
        cb_cV = cV.n

        y1_ld = [0] * NT
        pb_outv = [0] * NT
        pb_st = [0] * NT

        def _pbld0(q):
            q.wait_ge(sY1[0].h, sY1[0].n)
            q.wait_ge(sY1[1].h, sY1[1].n)
            q.wait_ge(sX1[0].h, sX1[0].n)
            q.wait_ge(sX1[1].h, sX1[1].n)
            q.wait_ge(cV.h, cb_cV)
            dma(q, sPB[0], y1t[:, 0, :], y1sp_h[0])
            dma(q, sPB[0], x1t[:, 0, :], x1sp_h[0])
        SY(_pbld0)
        y1_ld[0] = sPB[0].n

        for i in range(NT):
            bufi = i % 2
            if i + 1 < NT:
                def _ld(q, i=i):
                    if i >= 1:
                        q.wait_ge(cV.h, pb_outv[i - 1])
                    dma(q, sPB[(i + 1) % 2], y1t[:, (i + 1) % 2, :], y1sp_h[i + 1])
                    dma(q, sPB[(i + 1) % 2], x1t[:, (i + 1) % 2, :], x1sp_h[i + 1])
                SY(_ld)
                y1_ld[i + 1] = sPB[(i + 1) % 2].n

            def _comb(q, i=i):
                if i == 0:
                    q.wait_ge(sCCl.h, ccout_n)
                    q.wait_ge(cV.h, cb_cV)
                    q.wait_ge(sB2G.h, g2_n)
                q.wait_ge(cT.h, cT.n)
                q.wait_ge(cV.h, cV.n)
                q.tensor_tensor(out=comb8[:], in0=base_all[:, i, :], in1=cb_s[:],
                                op=OP.add).then_inc(cV.h, 1); cV.n += 1
            VE(_comb)
            comb_cV = cV.n

            def _bc(q, comb_cV=comb_cV):
                q.wait_ge(cV.h, comb_cV)
                q.matmul(pX[:, 0:8], bseltc[:], comb8[:], start=True,
                         stop=True).then_inc(cT.h, 1); cT.n += 1
            TE(_bc)
            bc_cT = cT.n

            def _wsel(q, i=i, bc_cT=bc_cT):
                q.wait_ge(cT.h, bc_cT)
                q.tensor_tensor(out=combf[:], in0=prefix_all[:, i, :], in1=pX[:, 0:8],
                                op=OP.add).then_inc(cV.h, 1); cV.n += 1
                q.wait_ge(cV.h, cV.n)
                q.tensor_tensor(out=combf[:], in0=combf[:], in1=onehot_all[:, i, :],
                                op=OP.mult).then_inc(cV.h, 1); cV.n += 1
                q.wait_ge(cV.h, cV.n)
                q.tensor_reduce(out=locs[:], in_=combf[:], axis=AX,
                                op=OP.add).then_inc(cV.h, 1); cV.n += 1
                q.wait_ge(cV.h, cV.n)
                q.tensor_single_scalar(out=keepv[:], in_=locs[:], scalar=float(CAP),
                                       op=OP.is_le).then_inc(cV.h, 1); cV.n += 1
                q.wait_ge(cV.h, cV.n)
                q.tensor_tensor(out=keepv[:], in0=keepv[:], in1=gval_all[:, i:i + 1],
                                op=OP.mult).then_inc(cV.h, 1); cV.n += 1
                q.wait_ge(cV.h, cV.n)
                q.tensor_scalar(out=wall[:], in0=onehot_all[:, i, :],
                                scalar1=keepv[:, 0:1], scalar2=None,
                                op0=OP.mult).then_inc(cV.h, 1); cV.n += 1
                q.memset(macc, 0.0).then_inc(cV.h, 1); cV.n += 1
            VE(_wsel)
            wsel_cV = cV.n

            def _wtr(q, wsel_cV=wsel_cV):
                q.wait_ge(cV.h, wsel_cV)
                q.transpose(pTf[0:8, 0:128], wall[:], identf[:]).then_inc(cT.h, 1)
                cT.n += 1
            TE(_wtr)
            wtr_cT = cT.n

            def _wtrev(q, wtr_cT=wtr_cT):
                q.wait_ge(cT.h, wtr_cT)
                q.tensor_copy(out=wallt[:], in_=pTf[0:8, 0:128]).then_inc(cV.h, 1)
                cV.n += 1
            VE(_wtrev)
            wallt_cV = cV.n

            def _ycast(q, i=i, bufi=bufi, wsel_cV=wsel_cV):
                q.wait_ge(sPB[i % 2].h, y1_ld[i])
                q.wait_ge(cV.h, wsel_cV)
                q.tensor_copy(out=xbf[:], in_=y1t[:, bufi, :]).then_inc(cV.h, 1)
                cV.n += 1
            VE(_ycast)
            yc_cV = cV.n

            def _ytr2(q, yc_cV=yc_cV):
                q.wait_ge(cV.h, yc_cV)
                q.wait_ge(cS.h, cS.n)
                for dc in range(8):
                    q.transpose(pT[:, 128 * dc:128 * dc + 128],
                                xbf[:, 128 * dc:128 * dc + 128],
                                identb[:]).then_inc(cT.h, 1); cT.n += 1
            TE(_ytr2)
            ytr2_cT = cT.n

            def _ytr2ev(q, ytr2_cT=ytr2_cT):
                q.wait_ge(cT.h, ytr2_cT)
                for dc in range(8):
                    q.copy(out=xfm[:, dc, :],
                           in_=pT[:, 128 * dc:128 * dc + 128]).then_inc(cS.h, 1)
                    cS.n += 1
            SE(_ytr2ev)
            yfm_cS = cS.n

            for e_ in range(E):
                p0, p1 = (pA, pB) if e_ % 2 == 0 else (pC, pD)

                def _app(q, e_=e_, p0=p0, p1=p1, yfm_cS=yfm_cS):
                    q.wait_ge(cS.h, yfm_cS)
                    q.wait_ge(cV.h, cV.n)
                    for half, pp in ((0, p0), (1, p1)):
                        for dc in range(8):
                            mmi = q.matmul(pp[:, 0:512], xfm[:, dc, :],
                                           w12_s[:, e_, dc, 512 * half:512 * half + 512],
                                           start=(dc == 0), stop=(dc == 7))
                    mmi.then_inc(cT.h, 1); cT.n += 1
                TE(_app)
                app_cT = cT.n

                def _acc(q, e_=e_, p0=p0, p1=p1, app_cT=app_cT, wsel_cV=wsel_cV):
                    q.wait_ge(cT.h, app_cT)
                    q.wait_ge(cV.h, max(wsel_cV, cV.n))
                    q.scalar_tensor_tensor(out=macc[:, 0:512], in0=p0[:, 0:512],
                                           scalar=wall[:, e_:e_ + 1],
                                           in1=macc[:, 0:512], op0=OP.mult,
                                           op1=OP.add).then_inc(cV.h, 1); cV.n += 1
                    q.scalar_tensor_tensor(out=macc[:, 512:1024], in0=p1[:, 0:512],
                                           scalar=wall[:, e_:e_ + 1],
                                           in1=macc[:, 512:1024], op0=OP.mult,
                                           op1=OP.add).then_inc(cV.h, 1); cV.n += 1
                VE(_acc)
            acc_cV = cV.n

            def _b12t(q, wallt_cV=wallt_cV):
                q.wait_ge(cV.h, wallt_cV)
                q.wait_ge(cV.h, cV.n)
                q.matmul(pX[:, 0:512], wallt[:], b12all[:, 0:512], start=True, stop=True)
                q.matmul(pTf[:, 0:512], wallt[:], b12all[:, 512:1024], start=True,
                         stop=True).then_inc(cT.h, 1); cT.n += 1
            TE(_b12t)
            b12t_cT = cT.n

            def _y2pre(q, bufi=bufi, acc_cV=acc_cV, b12t_cT=b12t_cT, i=i):
                q.wait_ge(cV.h, acc_cV)
                q.wait_ge(cT.h, b12t_cT)
                if i >= 1:
                    q.wait_ge(sOut[(i - 1) % 2].h, pb_st[i - 1])
                q.wait_ge(cV.h, cV.n)
                q.tensor_tensor(out=macc[:, 0:512], in0=macc[:, 0:512],
                                in1=pX[:, 0:512], op=OP.add).then_inc(cV.h, 1); cV.n += 1
                q.tensor_tensor(out=macc[:, 512:1024], in0=macc[:, 512:1024],
                                in1=pTf[:, 0:512], op=OP.add).then_inc(cV.h, 1); cV.n += 1
                q.wait_ge(cV.h, cV.n)
                q.tensor_tensor(out=y2t, in0=y1t[:, bufi, :], in1=macc,
                                op=OP.add).then_inc(cV.h, 1); cV.n += 1
                q.wait_ge(cV.h, cV.n)
                q.bn_stats(stats[:, 0:6], y2t[:, 0:512]).then_inc(cV.h, 1); cV.n += 1
                q.bn_stats(stats[:, 6:12], y2t[:, 512:1024]).then_inc(cV.h, 1); cV.n += 1
                q.wait_ge(cV.h, cV.n)
                q.bn_aggr(mv[:], stats[:]).then_inc(cV.h, 1); cV.n += 1
            VE(_y2pre)
            y2p_cV = cV.n

            def _rstd2(q, y2p_cV=y2p_cV):
                q.wait_ge(cV.h, y2p_cV)
                q.activation(out=rstd[:], in_=mv[:, 1:2], func=ACT.Ln,
                             bias=epsv[:, 0:1], scale=1.0).then_inc(cS.h, 1); cS.n += 1
                q.wait_ge(cS.h, cS.n)
                q.activation(out=rstd[:], in_=rstd[:], func=ACT.Exp,
                             scale=-0.5).then_inc(cS.h, 1); cS.n += 1
            SE(_rstd2)
            rstd2_cS = cS.n

            def _ln2fin(q, bufi=bufi, rstd2_cS=rstd2_cS):
                q.wait_ge(cS.h, rstd2_cS)
                q.tensor_scalar(out=y2t, in0=y2t, scalar1=mv[:, 0:1],
                                scalar2=rstd[:, 0:1], op0=OP.subtract,
                                op1=OP.mult).then_inc(cV.h, 1); cV.n += 1
                q.wait_ge(cV.h, cV.n)
                q.tensor_mul(y2t, y2t, g1r[:]).then_inc(cV.h, 1); cV.n += 1
                q.wait_ge(cV.h, cV.n)
                q.tensor_add(y2t, y2t, be1r[:]).then_inc(cV.h, 1); cV.n += 1
                q.wait_ge(cV.h, cV.n)
                q.tensor_tensor(out=y2t, in0=y2t, in1=x1t[:, bufi, :],
                                op=OP.add).then_inc(cV.h, 1); cV.n += 1
                q.wait_ge(cV.h, cV.n)
                q.bn_stats(stats[:, 0:6], y2t[:, 0:512]).then_inc(cV.h, 1); cV.n += 1
                q.bn_stats(stats[:, 6:12], y2t[:, 512:1024]).then_inc(cV.h, 1); cV.n += 1
                q.wait_ge(cV.h, cV.n)
                q.bn_aggr(mv[:], stats[:]).then_inc(cV.h, 1); cV.n += 1
            VE(_ln2fin)
            x2p_cV = cV.n

            def _rstd3(q, x2p_cV=x2p_cV):
                q.wait_ge(cV.h, x2p_cV)
                q.activation(out=rstd[:], in_=mv[:, 1:2], func=ACT.Ln,
                             bias=epsv[:, 0:1], scale=1.0).then_inc(cS.h, 1); cS.n += 1
                q.wait_ge(cS.h, cS.n)
                q.activation(out=rstd[:], in_=rstd[:], func=ACT.Exp,
                             scale=-0.5).then_inc(cS.h, 1); cS.n += 1
            SE(_rstd3)
            rstd3_cS = cS.n

            def _lnout(q, rstd3_cS=rstd3_cS):
                q.wait_ge(cS.h, rstd3_cS)
                q.tensor_scalar(out=y2t, in0=y2t, scalar1=mv[:, 0:1],
                                scalar2=rstd[:, 0:1], op0=OP.subtract,
                                op1=OP.mult).then_inc(cV.h, 1); cV.n += 1
            VE(_lnout)
            outv_cV = cV.n
            pb_outv[i] = cV.n

            def _stout(q, i=i, outv_cV=outv_cV):
                q.wait_ge(cV.h, outv_cV)
                dma(q, sOut[i % 2], out_h.transpose([1, 0, 2])[16 * i:16 * (i + 1)], y2t)
            SY(_stout)
            pb_st[i] = sOut[i % 2].n

        def _fin(q):
            for s_ in (sCst, sX0[0], sX0[1], sX1[0], sX1[1], sY1[0], sY1[1], sWa,
                       sSt, sW1, sW2, sCCl, sFst, sCin, sPB[0], sPB[1], sOut[0],
                       sOut[1], sB2G):
                q.wait_ge(s_.h, s_.n)
        GP(_fin)

    return nc


def _consts():
    ident_b = np.eye(128, dtype=ml_dtypes.bfloat16)
    ident_f = np.eye(128, dtype=np.float32)
    t1 = np.arange(128)
    n1, b1_ = t1 // 8, t1 % 8
    maskin = np.where((n1[:, None] == n1[None, :]), 0.0, NEG).astype(np.float32)
    t128 = ((b1_[:, None] == b1_[None, :]) & (n1[:, None] <= n1[None, :])
            ).astype(ml_dtypes.bfloat16)
    bsel = (b1_[:, None] == np.arange(8)[None, :]).astype(ml_dtypes.bfloat16)
    bselt = (np.arange(8)[:, None] == b1_[None, :]).astype(np.float32)
    onesf = np.ones((1, 128), np.float32)
    erow = np.broadcast_to(np.arange(8, dtype=np.float32), (128, 8)).copy()
    q16, k16 = np.meshgrid(np.arange(16), np.arange(16), indexing="ij")
    tri = (k16 <= q16).astype(ml_dtypes.bfloat16).reshape(1, 256)
    tri16 = np.broadcast_to(tri, (128, 256)).copy()
    return ident_b, ident_f, maskin, t128, bsel, bselt, onesf, erow, tri16


def _selc(core):
    cc, bb = np.meshgrid(np.arange(8), np.arange(8), indexing="ij")
    flat_c, flat_b = cc.reshape(-1), bb.reshape(-1)
    b = np.arange(8)
    sel = ((flat_b[:, None] < b[None, :]) |
           ((flat_b[:, None] == b[None, :]) & (flat_c[:, None] < core))
           ).astype(np.float32)
    return sel


def kernel(**inputs):
    gi = lambda k: np.ascontiguousarray(np.asarray(inputs[k], dtype=np.float32))
    x = gi("L0_attn_x")
    (ident_b, ident_f, maskin, t128, bsel, bselt, onesf, erow, tri16) = _consts()

    base = {
        "wq": gi("L0_attn_Wq"), "wk": gi("L0_attn_Wk"), "wv": gi("L0_attn_Wv"),
        "wo": gi("L0_attn_Wo"),
        "bq": gi("L0_attn_bq")[None, :], "bk": gi("L0_attn_bk")[None, :],
        "bv": gi("L0_attn_bv")[None, :], "bo": gi("L0_attn_bo")[None, :],
        "ipw": gi("L0_ffn_in_proj_w"), "ipb": gi("L0_ffn_in_proj_b"),
        "opw": gi("L0_ffn_out_proj_w"), "opb": gi("L0_ffn_out_proj_b")[None, :],
        "g1": gi("L0_ffn_ln1_g"), "be1": gi("L0_ffn_ln1_b"),
        "g2": gi("L0_ffn_ln2_g"), "be2": gi("L0_ffn_ln2_b"),
        "wg": gi("L0_ffn_wg"),
        "identb": ident_b, "identf": ident_f, "maskin": maskin,
        "t128": t128, "bsel": bsel, "bselt": bselt, "onesf": onesf, "erow": erow,
        "tri16": tri16,
    }
    w1 = gi("L0_ffn_w1"); b1 = gi("L0_ffn_b1")
    w2 = gi("L0_ffn_w2"); b2 = gi("L0_ffn_b2")

    in_maps = []
    for c in range(NC_):
        m = dict(base)
        m["x"] = np.ascontiguousarray(x[:, NLOC * c:NLOC * (c + 1), :])
        m["w1m"] = w1[c]
        m["w2m"] = w2[c]
        m["b1m"] = b1[c]
        m["b2m"] = np.ascontiguousarray(b2[c][None, :])
        m["selc"] = _selc(c)
        in_maps.append(m)

    nc = build()
    res = run_bass_kernel_spmd(nc, in_maps, core_ids=list(range(NC_)))
    out = np.concatenate([res.results[c]["out"] for c in range(NC_)], axis=1)
    return np.ascontiguousarray(out.astype(np.float32))


if __name__ == "__main__":
    import reference
    ins = reference.setup_inputs()
    ins = {k: np.asarray(v) for k, v in ins.items()}
    out = kernel(**ins)
    ref = np.asarray(reference.reference(**ins))
    err = np.mean((out - ref) ** 2) / np.mean(ref ** 2)
    print("resid var ratio:", err)


# version 10
# speedup vs baseline: 1.4466x; 1.4466x over previous
import numpy as np
import ml_dtypes
from contextlib import ExitStack

import concourse.bass as bass
import concourse.mybir as mybir
from concourse.bass_utils import run_bass_kernel_spmd
from concourse.replica_groups import maybe_share_collective_output_space
from concourse.hw_specs import get_activation_tables

F32 = mybir.dt.float32
BF16 = mybir.dt.bfloat16
U32 = mybir.dt.uint32
AX = mybir.AxisListType.X
OP = mybir.AluOpType
ACT = mybir.ActivationFunctionType

NC_ = 8
B = 8
SEQ = 4096
D = 1024
NLOC = SEQ // NC_
NT = 32
HO, CO = 16, 64
HI, HDI = 4, 256
E, FF, CAP = 8, 4096, 4096
NEG = -1.0e30


class Ctr:
    def __init__(self, es, nc, name):
        self.h = es.enter_context(nc.semaphore(name))
        self.n = 0


def build():
    nc = bass.Bass("TRN2", target_bir_lowering=False, debug=False,
                   detect_race_conditions=False)

    dt_ = nc.dram_tensor
    x_h = dt_("x", [B, NLOC, D], F32, kind="ExternalInput").ap()
    wq_h = dt_("wq", [D, D], F32, kind="ExternalInput").ap()
    wk_h = dt_("wk", [D, D], F32, kind="ExternalInput").ap()
    wv_h = dt_("wv", [D, D], F32, kind="ExternalInput").ap()
    wo_h = dt_("wo", [D, D], F32, kind="ExternalInput").ap()
    bq_h = dt_("bq", [1, D], F32, kind="ExternalInput").ap()
    bk_h = dt_("bk", [1, D], F32, kind="ExternalInput").ap()
    bv_h = dt_("bv", [1, D], F32, kind="ExternalInput").ap()
    bo_h = dt_("bo", [1, D], F32, kind="ExternalInput").ap()
    ipw_h = dt_("ipw", [3 * D, D], F32, kind="ExternalInput").ap()
    ipb_h = dt_("ipb", [3 * D], F32, kind="ExternalInput").ap()
    opw_h = dt_("opw", [D, D], F32, kind="ExternalInput").ap()
    opb_h = dt_("opb", [1, D], F32, kind="ExternalInput").ap()
    g1_h = dt_("g1", [D], F32, kind="ExternalInput").ap()
    be1_h = dt_("be1", [D], F32, kind="ExternalInput").ap()
    g2_h = dt_("g2", [D], F32, kind="ExternalInput").ap()
    be2_h = dt_("be2", [D], F32, kind="ExternalInput").ap()
    wg_h = dt_("wg", [D, E], F32, kind="ExternalInput").ap()
    w1m_h = dt_("w1m", [D, FF], F32, kind="ExternalInput").ap()
    w2m_h = dt_("w2m", [FF, D], F32, kind="ExternalInput").ap()
    b1m_h = dt_("b1m", [FF], F32, kind="ExternalInput").ap()
    b2m_h = dt_("b2m", [1, D], F32, kind="ExternalInput").ap()
    identb_h = dt_("identb", [128, 128], BF16, kind="ExternalInput").ap()
    identf_h = dt_("identf", [128, 128], F32, kind="ExternalInput").ap()
    maskin_h = dt_("maskin", [128, 128], F32, kind="ExternalInput").ap()
    t128_h = dt_("t128", [128, 128], BF16, kind="ExternalInput").ap()
    bsel_h = dt_("bsel", [128, 8], BF16, kind="ExternalInput").ap()
    bselt_h = dt_("bselt", [8, 128], F32, kind="ExternalInput").ap()
    selc_h = dt_("selc", [64, 8], F32, kind="ExternalInput").ap()
    onesf_h = dt_("onesf", [1, 128], F32, kind="ExternalInput").ap()
    erow_h = dt_("erow", [128, 8], F32, kind="ExternalInput").ap()
    tri16_h = dt_("tri16", [128, 256], BF16, kind="ExternalInput").ap()

    out_h = dt_("out", [B, NLOC, D], F32, kind="ExternalOutput").ap()

    x1sp_h = dt_("x1sp", [NT, 128, D], F32, kind="Internal").ap()
    y1sp_h = dt_("y1sp", [NT, 128, D], F32, kind="Internal").ap()
    RG = [[i for i in range(NC_)]]
    sp_f = maybe_share_collective_output_space("AllGather", RG)
    ccf_in = dt_("ccf_in", [1, 64 + D], F32, kind="Internal").ap()
    ccf_out = dt_("ccf_out", [NC_, 64 + D], F32, kind="Internal",
                  addr_space=sp_f).ap()
    ccw_in = dt_("ccw_in", [128, 8 * D], BF16, kind="Internal").ap()
    ccw_out = dt_("ccw_out", [NC_, 128, 8 * D], BF16, kind="Internal",
                  addr_space=sp_f).ap()

    es = ExitStack()
    sb = lambda name, shape, dtype: es.enter_context(nc.sbuf_tensor(name, shape, dtype))
    ps_ = lambda name, shape, dtype=F32: es.enter_context(nc.psum_tensor(name, shape, dtype))

    arena = sb("arena", [128, 65536], BF16)
    wq_s = arena[:, 0:8192].rearrange("p (c n) -> p c n", c=8)
    wk_s = arena[:, 8192:16384].rearrange("p (c n) -> p c n", c=8)
    wv_s = arena[:, 16384:24576].rearrange("p (c n) -> p c n", c=8)
    wo_s = arena[:, 24576:32768].rearrange("p (c n) -> p c n", c=8)
    ipwt_s = arena[:, 32768:57344].rearrange("p (c n) -> p c n", c=8)
    opwt_s = arena[:, 57344:65536].rearrange("p (c n) -> p c n", c=8)
    w1t_s = arena[:, 0:32768].rearrange("p (c n) -> p c n", c=32)
    w2f_s = arena[:, 32768:65536].rearrange("p (c n) -> p c n", c=32)
    w12_s = arena[:, 0:65536].rearrange("p (e c n) -> p e c n", e=8, c=8)

    g1r = sb("g1r", [128, D], F32)
    be1r = sb("be1r", [128, D], F32)
    wgc = sb("wgc", [128, 8, E], F32)
    identb = sb("identb_s", [128, 128], BF16)
    identf = sb("identf_s", [128, 128], F32)
    maskin = sb("maskin_s", [128, 128], F32)
    t128c = sb("t128c", [128, 128], BF16)
    bselc = sb("bselc", [128, 8], BF16)
    bseltc = sb("bseltc", [8, 128], F32)
    selcc = sb("selcc", [64, 8], F32)
    onesf = sb("onesf_s", [128, 128], F32)
    erow = sb("erow_s", [128, 8], F32)
    tri16_sb = sb("tri16_sb", [128, 256], BF16)
    tri16 = tri16_sb[:].rearrange("p (a b) -> p a b", a=16)
    epsv = sb("epsv", [128, 1], F32)
    brows = sb("brows", [128, 2 * D], F32)
    bcolq = sb("bcolq", [128, 24], F32)
    bcoli = sb("bcoli", [128, 24], F32)
    b1c = sb("b1c", [128, 32], BF16)

    onehot_all = sb("onehot_all", [128, NT, E], BF16)
    prefix_all = sb("prefix_all", [128, NT, E], F32)
    gval_all = sb("gval_all", [128, NT], F32)
    base_all = sb("base_all", [8, NT, E], F32)
    pc_s = sb("pc_s", [8, 8], F32)
    cb_s = sb("cb_s", [8, 8], F32)
    runc = sb("runc", [8, 8], F32)

    xa = sb("xa", [128, 2, D], F32)
    x1t = xa
    y1t = sb("y1t", [128, 2, D], F32)
    xbf = sb("xbf", [128, D], BF16)
    xfm = sb("xfm", [128, 8, 128], BF16)
    qfmst = sb("qfmst", [128, 4, 128], BF16)
    fm32 = sb("fm32", [128, 128], F32)
    qtm = sb("qtm", [128, D], BF16)
    ktm = sb("ktm", [128, D], BF16)
    vtm = sb("vtm", [128, D], BF16)
    BOFF = (0, 256, 768, 1536)
    PK = 2560
    sbuf_S = sb("sbuf_S", [128, 2 * PK], BF16)
    sbuf_E = sb("sbuf_E", [128, 2 * PK], BF16)

    def sbview(k, j):
        kk = 16 * (j + 1)
        buf = (sbuf_S, sbuf_E)[k]
        return buf[:].rearrange("p (h n) -> p h n", h=2)[
            :, :, BOFF[j]:BOFF[j] + 16 * kk].rearrange(
            "p h (a b) -> p h a b", a=16)
    ddn = sb("ddn", [128, 256], BF16)
    dd4 = ddn[:].rearrange("p (k h q) -> p k h q", k=2, h=2)
    dnm = sb("dnm", [128, 256], BF16)
    dn4 = dnm[:].rearrange("p (k h q) -> p k h q", k=2, h=2)
    rec = sb("rec", [128, 128], F32)
    recb = sb("recb", [128, 128], BF16)
    recb2 = recb[:].rearrange("p (h q) -> p h q", h=2)
    bnd = sb("bnd", [128, 4], F32)
    attn = sb("attn", [128, D], BF16)
    sco = sb("sco", [128, 128], F32)
    ei_s = sb("ei_s", [128, 128], BF16)
    eit_s = sb("eit_s", [128, 128], BF16)
    otm = attn
    stats = sb("stats", [128, 12], F32)
    mv = sb("mv", [128, 2], F32)
    rstd = sb("rstd", [128, 1], F32)
    lg = sb("lg", [128, E], F32)
    eg = sb("eg", [128, E], F32)
    sume = sb("sume", [128, 1], F32)
    gts = sb("gts", [128, E], F32)
    gmx = sb("gmx", [128, 8], F32)
    gmi = sb("gmi", [128, 8], U32)
    idxf = sb("idxf", [128, 1], F32)
    combf = sb("combf", [128, E], F32)
    locs = sb("locs", [128, 1], F32)
    keepv = sb("keepv", [128, 1], F32)
    wall = sb("wall", [128, E], F32)
    wallt = sb("wallt", [8, 128], F32)
    comb8 = sb("comb8", [8, 8], F32)

    qkfm = sbuf_S[:, 0:2048].rearrange(
        "p (c n) -> p c n", c=16)
    S_f32 = sbuf_S[:].bitcast(F32)
    E_f32 = sbuf_E[:].bitcast(F32)
    macc = S_f32[:, 0:1024]
    y2t = E_f32[:, 0:1024]
    b12all = E_f32[0:8, 1024:2048]
    b12mine = S_f32[0:1, 0:1024]
    b2st = S_f32[0:1, 1024:2048]
    w1raw = xbf
    w12st = ktm[:, 0:512]

    pA = ps_("pA", [128, 512])
    pB = ps_("pB", [128, 512])
    pC = ps_("pC", [128, 512])
    pD = ps_("pD", [128, 512])
    pT = ps_("pT", [128, 1024], BF16)
    pTf = ps_("pTf", [128, 512])
    pX = ps_("pX", [128, 512])
    pCNT = ps_("pCNT", [128, 512])

    blkmgr = es.enter_context(nc.Block())

    cT = Ctr(es, nc, "cT")
    cV = Ctr(es, nc, "cV")
    cS = Ctr(es, nc, "cS")
    cG = Ctr(es, nc, "cG")
    cCC = Ctr(es, nc, "cCC")
    sCst = Ctr(es, nc, "sCst")
    sX0 = [Ctr(es, nc, "sX0a"), Ctr(es, nc, "sX0b")]
    sX1 = [Ctr(es, nc, "sX1a"), Ctr(es, nc, "sX1b")]
    sY1 = [Ctr(es, nc, "sY1a"), Ctr(es, nc, "sY1b")]
    sWa = Ctr(es, nc, "sWa")
    sSt = Ctr(es, nc, "sSt")
    sW1 = Ctr(es, nc, "sW1")
    sW2 = Ctr(es, nc, "sW2")
    sCCl = Ctr(es, nc, "sCCl")
    sFst = Ctr(es, nc, "sFst")
    sCin = Ctr(es, nc, "sCin")
    sPB = [Ctr(es, nc, "sPBa"), Ctr(es, nc, "sPBb")]
    sOut = [Ctr(es, nc, "sOuta"), Ctr(es, nc, "sOutb")]
    sB2G = Ctr(es, nc, "sB2G")

    def SY(f):
        @blkmgr.sync
        def _(q):
            f(q)

    def TE(f):
        @blkmgr.tensor
        def _(q):
            f(q)

    def VE(f):
        @blkmgr.vector
        def _(q):
            f(q)

    def SE(f):
        @blkmgr.scalar
        def _(q):
            f(q)

    def GP(f):
        @blkmgr.gpsimd
        def _(q):
            f(q)

    def dma(q, sem, out, in_, slow=False):
        q.dma_start(out=out, in_=in_, allow_slow_non_contiguous=slow
                    ).then_inc(sem.h, 16)
        sem.n += 16

    with es:
        def _pre_sync(q):
            dma(q, sCst, identb[:], identb_h[:])
            dma(q, sCst, identf[:], identf_h[:])
            dma(q, sCst, maskin[:], maskin_h[:])
            dma(q, sCst, t128c[:], t128_h[:])
            dma(q, sCst, bselc[:], bsel_h[:])
            dma(q, sCst, bseltc[:], bselt_h[:])
            dma(q, sCst, selcc[:], selc_h[:])
            dma(q, sCst, onesf[:], onesf_h[0, :][None, :].partition_broadcast(128))
            dma(q, sCst, erow[:], erow_h[:])
            dma(q, sCst, tri16_sb[:], tri16_h[:])
            dma(q, sCst, g1r[:], g1_h[None, :].partition_broadcast(128))
            dma(q, sCst, be1r[:], be1_h[None, :].partition_broadcast(128))
            dma(q, sCst, wgc[:], wg_h.rearrange("(c p) e -> p c e", p=128))
            dma(q, sCst, brows[0:1, 0:D], bo_h[:])
            dma(q, sCst, brows[0:1, D:2 * D], bk_h[:])
            dma(q, sCst, brows[32:33, 0:D], opb_h[:])
            dma(q, sCst, brows[64:65, 0:D], bq_h[:])
            dma(q, sCst, bcolq[:, 0:8], bq_h[0, :].rearrange("(c p) -> p c", p=128), slow=True)
            dma(q, sCst, bcolq[:, 8:16], bk_h[0, :].rearrange("(c p) -> p c", p=128), slow=True)
            dma(q, sCst, bcolq[:, 16:24], bv_h[0, :].rearrange("(c p) -> p c", p=128), slow=True)
            dma(q, sCst, bcoli[:], ipb_h.rearrange("(c p) -> p c", p=128), slow=True)
        SY(_pre_sync)

        def _pre_gp(q):
            for w_s, w_hh in ((wq_s, wq_h), (wk_s, wk_h), (wv_s, wv_h), (wo_s, wo_h)):
                dma(q, sWa, w_s, w_hh.rearrange("(c p) n -> p c n", p=128))
            dma(q, sWa, b1c[:], b1m_h.rearrange("(c p) -> p c", p=128), slow=True)
        GP(_pre_gp)
        pre_w = sWa.n
        pre_cst = sCst.n

        def _pre_se(q):
            for idx, funcs in enumerate(get_activation_tables(nc.m.arch).values()):
                if ACT.Exp in funcs and ACT.Ln in funcs and ACT.Identity in funcs:
                    q.add_instruction(mybir.InstLoadActFuncSet(
                        name=nc.get_next_instruction_name(),
                        act_func_set_id=idx, ins=[], outs=[]))
                    break
        SE(_pre_se)

        def _pre_v(q):
            q.memset(epsv[:], 1e-5)
            q.memset(base_all[:, 0, :], 0.0)
            q.memset(runc[:], 0.0).then_inc(cV.h, 1); cV.n += 1
        VE(_pre_v)

        for kind in range(2):
            ntile = 24 if kind == 0 else 8
            src_h = ipw_h if kind == 0 else opw_h
            dst_s = ipwt_s if kind == 0 else opwt_s
            for it in range(ntile):
                def _ld(q, it=it, src_h=src_h):
                    q.wait_ge(cS.h, cS.n)
                    dma(q, sSt, w1raw[:], src_h[128 * it:128 * (it + 1), :])
                GP(_ld)
                ld_n = sSt.n

                def _tr(q, ld_n=ld_n):
                    q.wait_ge(sSt.h, ld_n)
                    q.wait_ge(sCst.h, pre_cst)
                    q.wait_ge(cS.h, cS.n)
                    for dc in range(8):
                        q.transpose(pT[:, 128 * dc:128 * dc + 128],
                                    w1raw[:, 128 * dc:128 * dc + 128],
                                    identb[:]).then_inc(cT.h, 1); cT.n += 1
                TE(_tr)
                tr_n = cT.n

                def _ev(q, it=it, tr_n=tr_n, dst_s=dst_s):
                    q.wait_ge(cT.h, tr_n)
                    for dc in range(8):
                        q.copy(out=dst_s[:, dc, 128 * it:128 * (it + 1)],
                               in_=pT[:, 128 * dc:128 * dc + 128]
                               ).then_inc(cS.h, 1); cS.n += 1
                SE(_ev)
        wT_built_cS = cS.n
        wT_built_cT = cT.n

        x0_loaded = [0] * NT
        tile_done_cV = [0] * NT
        x1_stored = [0] * NT
        y1_stored = [0] * NT

        def _ld0(q):
            dma(q, sX0[0], xa[:, 0, :], x_h.transpose([1, 0, 2])[0:16])
        SY(_ld0)
        x0_loaded[0] = sX0[0].n

        for i in range(NT):
            bufi = i % 2
            if i + 1 < NT:
                def _ld(q, i=i):
                    if i >= 1:
                        q.wait_ge(cV.h, tile_done_cV[i - 1])
                        q.wait_ge(sX1[(i - 1) % 2].h, x1_stored[i - 1])
                    dma(q, sX0[(i + 1) % 2], xa[:, (i + 1) % 2, :],
                        x_h.transpose([1, 0, 2])[16 * (i + 1):16 * (i + 2)])
                SY(_ld)
                x0_loaded[i + 1] = sX0[(i + 1) % 2].n

            def _cast(q, i=i, bufi=bufi):
                q.wait_ge(sX0[i % 2].h, x0_loaded[i])
                if i == 0:
                    q.wait_ge(sCst.h, pre_cst)
                    q.wait_ge(sWa.h, pre_w)
                    q.wait_ge(cT.h, wT_built_cT)
                q.tensor_copy(out=xbf[:], in_=xa[:, bufi, :]).then_inc(cV.h, 1); cV.n += 1
            VE(_cast)
            cast_cV = cV.n

            def _tr(q, i=i, cast_cV=cast_cV):
                q.wait_ge(cV.h, cast_cV)
                if i == 0:
                    q.wait_ge(cS.h, wT_built_cS)
                q.wait_ge(cS.h, cS.n)
                for dc in range(8):
                    q.transpose(pT[:, 128 * dc:128 * dc + 128],
                                xbf[:, 128 * dc:128 * dc + 128],
                                identb[:]).then_inc(cT.h, 1); cT.n += 1
            TE(_tr)
            tr_cT = cT.n

            def _trev(q, tr_cT=tr_cT):
                q.wait_ge(cT.h, tr_cT)
                for dc in range(8):
                    q.copy(out=xfm[:, dc, :],
                           in_=pT[:, 128 * dc:128 * dc + 128]).then_inc(cS.h, 1); cS.n += 1
            SE(_trev)
            xfm_cS = cS.n

            def _qkmm_o(q, xfm_cS=xfm_cS):
                q.wait_ge(cS.h, xfm_cS)
                q.wait_ge(cV.h, cV.n)
                banks = (pA, pB, pC, pD)
                for dc in range(8):
                    for bi, (w_s, half) in enumerate(((wq_s, 0), (wq_s, 1),
                                                      (wk_s, 0), (wk_s, 1))):
                        q.matmul(banks[bi][:, 0:512], xfm[:, dc, :],
                                 w_s[:, dc, 512 * half:512 * half + 512],
                                 start=(dc == 0), stop=False)
                for bi, (brow, boff) in enumerate(((64, 0), (64, 512), (0, D), (0, D + 512))):
                    mmi = q.matmul(banks[bi][:, 0:512], onesf[brow:brow + 1, :],
                                   brows[brow:brow + 1, boff:boff + 512],
                                   start=False, stop=True)
                mmi.then_inc(cT.h, 1); cT.n += 1
            TE(_qkmm_o)
            qko_cT = cT.n

            def _qkev_o(q, qko_cT=qko_cT):
                q.wait_ge(cT.h, qko_cT)
                q.tensor_copy(out=qtm[:, 0:512], in_=pA[:, 0:512]).then_inc(cV.h, 1); cV.n += 1
                q.tensor_copy(out=qtm[:, 512:1024], in_=pB[:, 0:512]).then_inc(cV.h, 1); cV.n += 1
                q.tensor_copy(out=ktm[:, 0:512], in_=pC[:, 0:512]).then_inc(cV.h, 1); cV.n += 1
                q.tensor_copy(out=ktm[:, 512:1024], in_=pD[:, 0:512]).then_inc(cV.h, 1); cV.n += 1
            VE(_qkev_o)
            qkev_cV = cV.n

            for vg in range(2):
                def _vmm_o(q, vg=vg, xfm_cS=xfm_cS, qkev_cV=qkev_cV):
                    q.wait_ge(cS.h, max(xfm_cS, cS.n))
                    q.wait_ge(cV.h, qkev_cV)
                    for k_ in range(4):
                        vc = 4 * vg + k_
                        pp = (pA, pB, pC, pD)[k_]
                        for dc in range(8):
                            mmi = q.matmul(pp[:, 0:128],
                                           wv_s[:, dc, 128 * vc:128 * vc + 128],
                                           xfm[:, dc, :], start=(dc == 0),
                                           stop=(dc == 7))
                    mmi.then_inc(cT.h, 1); cT.n += 1
                TE(_vmm_o)
                vo_cT = cT.n

                def _vev_o(q, vg=vg, vo_cT=vo_cT):
                    q.wait_ge(cT.h, vo_cT)
                    for k_ in range(4):
                        vc = 4 * vg + k_
                        q.activation(out=qfmst[:, k_, :], in_=(pA, pB, pC, pD)[k_][:, 0:128],
                                     func=ACT.Identity, bias=bcolq[:, 16 + vc:16 + vc + 1],
                                     scale=1.0).then_inc(cS.h, 1); cS.n += 1
                SE(_vev_o)
                vev_cS = cS.n

                def _vtr_o(q, vev_cS=vev_cS):
                    q.wait_ge(cS.h, vev_cS)
                    q.wait_ge(cV.h, cV.n)
                    for k_ in range(4):
                        q.transpose(pT[:, 128 * k_:128 * k_ + 128], qfmst[:, k_, :],
                                    identb[:]).then_inc(cT.h, 1); cT.n += 1
                TE(_vtr_o)
                vtr_cT = cT.n

                def _vtrev_o(q, vg=vg, vtr_cT=vtr_cT):
                    q.wait_ge(cT.h, vtr_cT)
                    for k_ in range(4):
                        vc = 4 * vg + k_
                        q.tensor_copy(out=vtm[:, 128 * vc:128 * vc + 128],
                                      in_=pT[:, 128 * k_:128 * k_ + 128]
                                      ).then_inc(cV.h, 1); cV.n += 1
                VE(_vtrev_o)
            qkv_cV = cV.n

            def _bnd(q, qkv_cV=qkv_cV):
                q.wait_ge(cV.h, qkv_cV)
                q.reduce_max(bnd[:, 0:1], qtm[:, :], axis=AX,
                             apply_absolute_value=True).then_inc(cV.h, 1); cV.n += 1
                q.reduce_max(bnd[:, 1:2], ktm[:, :], axis=AX,
                             apply_absolute_value=True).then_inc(cV.h, 1); cV.n += 1
                q.wait_ge(cV.h, cV.n)
                q.tensor_tensor(out=bnd[:, 2:3], in0=bnd[:, 0:1], in1=bnd[:, 1:2],
                                op=OP.mult).then_inc(cV.h, 1); cV.n += 1
                q.wait_ge(cV.h, cV.n)
                q.tensor_scalar_mul(bnd[:, 3:4], bnd[:, 2:3], -0.125
                                    ).then_inc(cV.h, 1); cV.n += 1
            VE(_bnd)
            bnd_cV = cV.n

            QB = [(0, 16), (16, 32), (32, 48), (48, 64)]
            NG = 8
            tri_cVn = [0] * NG
            den_blk = [[0] * 4 for _ in range(NG)]
            exp_cSn = [0] * NG
            ev_cGn = [0] * NG

            def hv2(buf, g):
                return buf[:, 128 * g:128 * (g + 1)].rearrange(
                    "p (h c) -> p h c", h=2)

            lp = lambda: nc.allow_low_precision(
                reason="fp32-accumulated reduce, single bf16 round")

            for g in range(NG + 2):
                def _v(q, g=g):
                    gg2 = g - 2
                    k2 = gg2 % 2
                    if g >= 2:
                        q.wait_ge(cG.h, ev_cGn[gg2])
                        with lp():
                            for j, (qa, qb) in enumerate(QB):
                                q.tensor_reduce(out=dn4[:, k2, :, qa:qb],
                                                in_=sbview(k2, j),
                                                axis=AX, op=OP.add
                                                ).then_inc(cV.h, 1); cV.n += 1
                        n_num = cV.n
                        q.wait_ge(cV.h, den_blk[gg2][3])
                        q.tensor_copy(out=rec[:], in_=ddn[:, 128 * k2:128 * k2 + 128]
                                      ).then_inc(cV.h, 1); cV.n += 1
                        n_c1 = cV.n
                    if 1 <= g <= NG:
                        gg = g - 1
                        k = gg % 2
                        q.wait_ge(cS.h, exp_cSn[gg])
                        with lp():
                            for j, (qa, qb) in enumerate(QB):
                                q.tensor_reduce(out=dd4[:, k, :, qa:qb],
                                                in_=sbview(k, j),
                                                axis=AX, op=OP.add
                                                ).then_inc(cV.h, 1); cV.n += 1
                                den_blk[gg][j] = cV.n
                    if g >= 2:
                        q.wait_ge(cV.h, n_c1)
                        q.reciprocal(rec[:], rec[:]).then_inc(cV.h, 1); cV.n += 1
                        n_r1 = cV.n
                    if g < NG:
                        k = g % 2
                        qv = hv2(qtm, g)
                        kv = hv2(ktm, g)
                        for j, (qa, qb) in enumerate(QB):
                            kk = 16 * (j + 1)
                            q.tensor_tensor(
                                out=sbview(k, j),
                                in0=qv[:, :, qa:qb, None].broadcast_to([128, 2, 16, kk]),
                                in1=kv[:, :, None, 0:kk].broadcast_to([128, 2, 16, kk]),
                                op=OP.mult).then_inc(cV.h, 1); cV.n += 1
                    if g >= 2:
                        q.wait_ge(cV.h, n_r1)
                        q.tensor_copy(out=recb[:], in_=rec[:]
                                      ).then_inc(cV.h, 1); cV.n += 1
                    if g < NG:
                        k = g % 2
                        q.wait_ge(cV.h, cV.n)
                        for j in range(4):
                            dv = sbview(k, j)[:, :, :, 16 * j:16 * j + 16]
                            q.tensor_tensor(
                                out=dv, in0=dv,
                                in1=tri16[:, None, :, :].broadcast_to([128, 2, 16, 16]),
                                op=OP.add).then_inc(cV.h, 1); cV.n += 1
                    if g >= 2:
                        if g >= NG:
                            q.wait_ge(cV.h, cV.n)
                        q.tensor_tensor(out=hv2(attn, gg2), in0=dn4[:, k2],
                                        in1=recb2, op=OP.mult
                                        ).then_inc(cV.h, 1); cV.n += 1
                VE(_v)
                if g < NG:
                    tri_cVn[g] = cV.n - (1 if g >= 2 else 0)

                if 1 <= g <= NG:
                    def _gev(q, g=g):
                        gg = g - 1
                        k = gg % 2
                        vv = hv2(vtm, gg)
                        for j, (qa, qb) in enumerate(QB):
                            kk = 16 * (j + 1)
                            q.wait_ge(cV.h, den_blk[gg][j])
                            q.tensor_tensor(
                                out=sbview(k, j), in0=sbview(k, j),
                                in1=vv[:, :, None, 0:kk].broadcast_to([128, 2, 16, kk]),
                                op=OP.mult).then_inc(cG.h, 1); cG.n += 1
                    GP(_gev)
                    ev_cGn[g - 1] = cG.n

                if g < NG:
                    def _sexp(q, g=g):
                        k = g % 2
                        q.wait_ge(cV.h, tri_cVn[g])
                        for j in range(4):
                            q.activation(out=sbview(k, j), in_=sbview(k, j),
                                         func=ACT.Exp, bias=bnd[:, 3:4],
                                         scale=0.125).then_inc(cS.h, 1); cS.n += 1
                    SE(_sexp)
                    exp_cSn[g] = cS.n
            attn_cV = cV.n

            def _atr(q, attn_cV=attn_cV):
                q.wait_ge(cV.h, attn_cV)
                q.wait_ge(cS.h, cS.n)
                for dc in range(8):
                    q.transpose(pT[:, 128 * dc:128 * dc + 128],
                                attn[:, 128 * dc:128 * dc + 128],
                                identb[:]).then_inc(cT.h, 1); cT.n += 1
            TE(_atr)
            atr_cT = cT.n

            def _atrev(q, atr_cT=atr_cT):
                q.wait_ge(cT.h, atr_cT)
                for dc in range(8):
                    q.copy(out=xfm[:, dc, :],
                           in_=pT[:, 128 * dc:128 * dc + 128]).then_inc(cS.h, 1); cS.n += 1
            SE(_atrev)
            afm_cS = cS.n

            def _womm(q, afm_cS=afm_cS):
                q.wait_ge(cS.h, afm_cS)
                q.wait_ge(cV.h, cV.n)
                for half, pp in ((0, pA), (1, pB)):
                    for dc in range(8):
                        q.matmul(pp[:, 0:512], xfm[:, dc, :],
                                 wo_s[:, dc, 512 * half:512 * half + 512],
                                 start=(dc == 0), stop=False)
                    q.matmul(pp[:, 0:512], onesf[0:1, :],
                             brows[0:1, 512 * half:512 * half + 512],
                             start=False, stop=True).then_inc(cT.h, 1); cT.n += 1
            TE(_womm)
            wo_cT = cT.n

            def _x1pre(q, bufi=bufi, wo_cT=wo_cT, i=i):
                q.wait_ge(cT.h, wo_cT)
                if i >= 2:
                    q.wait_ge(sX1[i % 2].h, x1_stored[i - 2])
                q.tensor_tensor(out=xa[:, bufi, 0:512], in0=xa[:, bufi, 0:512],
                                in1=pA[:, 0:512], op=OP.add).then_inc(cV.h, 1); cV.n += 1
                q.tensor_tensor(out=xa[:, bufi, 512:1024], in0=xa[:, bufi, 512:1024],
                                in1=pB[:, 0:512], op=OP.add).then_inc(cV.h, 1); cV.n += 1
                q.wait_ge(cV.h, cV.n)
                q.bn_stats(stats[:, 0:6], x1t[:, bufi, 0:512]).then_inc(cV.h, 1); cV.n += 1
                q.bn_stats(stats[:, 6:12], x1t[:, bufi, 512:1024]).then_inc(cV.h, 1); cV.n += 1
                q.wait_ge(cV.h, cV.n)
                q.bn_aggr(mv[:], stats[:]).then_inc(cV.h, 1); cV.n += 1
            VE(_x1pre)
            x1pre_cV = cV.n

            def _rstd0(q, x1pre_cV=x1pre_cV):
                q.wait_ge(cV.h, x1pre_cV)
                q.activation(out=rstd[:], in_=mv[:, 1:2], func=ACT.Ln,
                             bias=epsv[:, 0:1], scale=1.0).then_inc(cS.h, 1); cS.n += 1
                q.wait_ge(cS.h, cS.n)
                q.activation(out=rstd[:], in_=rstd[:], func=ACT.Exp,
                             scale=-0.5).then_inc(cS.h, 1); cS.n += 1
            SE(_rstd0)
            rstd0_cS = cS.n

            def _ln0(q, bufi=bufi, rstd0_cS=rstd0_cS):
                q.wait_ge(cS.h, rstd0_cS)
                q.tensor_scalar(out=x1t[:, bufi, :], in0=x1t[:, bufi, :],
                                scalar1=mv[:, 0:1], scalar2=rstd[:, 0:1],
                                op0=OP.subtract, op1=OP.mult).then_inc(cV.h, 1); cV.n += 1
            VE(_ln0)
            x1_cV = cV.n

            def _stx1(q, i=i, bufi=bufi, x1_cV=x1_cV):
                q.wait_ge(cV.h, x1_cV)
                dma(q, sX1[i % 2], x1sp_h[i], x1t[:, bufi, :])
            SY(_stx1)
            x1_stored[i] = sX1[i % 2].n

            def _cast2(q, bufi=bufi, x1_cV=x1_cV):
                q.wait_ge(cV.h, x1_cV)
                q.tensor_copy(out=xbf[:], in_=x1t[:, bufi, :]).then_inc(cV.h, 1); cV.n += 1
            VE(_cast2)
            cast2_cV = cV.n

            def _tr2(q, cast2_cV=cast2_cV):
                q.wait_ge(cV.h, cast2_cV)
                q.wait_ge(cS.h, cS.n)
                for dc in range(8):
                    q.transpose(pT[:, 128 * dc:128 * dc + 128],
                                xbf[:, 128 * dc:128 * dc + 128],
                                identb[:]).then_inc(cT.h, 1); cT.n += 1
            TE(_tr2)
            tr2_cT = cT.n

            def _tr2ev(q, tr2_cT=tr2_cT):
                q.wait_ge(cT.h, tr2_cT)
                for dc in range(8):
                    q.copy(out=xfm[:, dc, :],
                           in_=pT[:, 128 * dc:128 * dc + 128]).then_inc(cS.h, 1); cS.n += 1
            SE(_tr2ev)
            x1fm_cS = cS.n

            for icg in range(4):
                def _qkmm(q, icg=icg, x1fm_cS=x1fm_cS):
                    q.wait_ge(cS.h, x1fm_cS)
                    q.wait_ge(cS.h, cS.n)
                    for k_ in range(4):
                        ic = 4 * icg + k_
                        pp = (pA, pB, pC, pD)[k_]
                        for dc in range(8):
                            mmi = q.matmul(pp[:, 0:128],
                                           ipwt_s[:, dc, 128 * ic:128 * ic + 128],
                                           xfm[:, dc, :], start=(dc == 0),
                                           stop=(dc == 7))
                    mmi.then_inc(cT.h, 1); cT.n += 1
                TE(_qkmm)
                qk_cT = cT.n

                def _qkev(q, icg=icg, qk_cT=qk_cT):
                    q.wait_ge(cT.h, qk_cT)
                    for k_ in range(4):
                        ic = 4 * icg + k_
                        pp = (pA, pB, pC, pD)[k_]
                        q.activation(out=qkfm[:, ic, :], in_=pp[:, 0:128],
                                     func=ACT.Identity, bias=bcoli[:, ic:ic + 1],
                                     scale=1.0).then_inc(cS.h, 1); cS.n += 1
                SE(_qkev)
            qkfm_cS = cS.n

            for vg in range(2):
                def _vmm(q, vg=vg, x1fm_cS=x1fm_cS):
                    q.wait_ge(cS.h, x1fm_cS)
                    q.wait_ge(cS.h, cS.n)
                    for k_ in range(4):
                        vc = 4 * vg + k_
                        pp = (pA, pB, pC, pD)[k_]
                        for dc in range(8):
                            mmi = q.matmul(pp[:, 0:128],
                                           ipwt_s[:, dc, 2048 + 128 * vc:2048 + 128 * vc + 128],
                                           xfm[:, dc, :], start=(dc == 0),
                                           stop=(dc == 7))
                    mmi.then_inc(cT.h, 1); cT.n += 1
                TE(_vmm)
                vmm_cT = cT.n

                def _vev(q, vg=vg, vmm_cT=vmm_cT):
                    q.wait_ge(cT.h, vmm_cT)
                    for k_ in range(4):
                        vc = 4 * vg + k_
                        q.activation(out=qfmst[:, k_, :], in_=(pA, pB, pC, pD)[k_][:, 0:128],
                                     func=ACT.Identity, bias=bcoli[:, 16 + vc:16 + vc + 1],
                                     scale=1.0).then_inc(cS.h, 1); cS.n += 1
                SE(_vev)
                vev_cS = cS.n

                def _vtr(q, vev_cS=vev_cS):
                    q.wait_ge(cS.h, vev_cS)
                    q.wait_ge(cV.h, cV.n)
                    for k_ in range(4):
                        q.transpose(pT[:, 128 * k_:128 * k_ + 128], qfmst[:, k_, :],
                                    identb[:]).then_inc(cT.h, 1); cT.n += 1
                TE(_vtr)
                vtr_cT = cT.n

                def _vtrev(q, vg=vg, vtr_cT=vtr_cT):
                    q.wait_ge(cT.h, vtr_cT)
                    for k_ in range(4):
                        vc = 4 * vg + k_
                        q.tensor_copy(out=vtm[:, 128 * vc:128 * vc + 128],
                                      in_=pT[:, 128 * k_:128 * k_ + 128]
                                      ).then_inc(cV.h, 1); cV.n += 1
                VE(_vtrev)
            vtm_cV = cV.n

            for h in range(HI):
                def _scmm(q, h=h, qkfm_cS=qkfm_cS):
                    q.wait_ge(cS.h, qkfm_cS)
                    q.wait_ge(cV.h, cV.n)
                    q.matmul(pTf[:, 0:128], qkfm[:, 2 * h, :], qkfm[:, 8 + 2 * h, :],
                             start=True, stop=False)
                    q.matmul(pTf[:, 0:128], qkfm[:, 2 * h + 1, :],
                             qkfm[:, 8 + 2 * h + 1, :], start=False,
                             stop=True).then_inc(cT.h, 1); cT.n += 1
                TE(_scmm)
                sc_cT = cT.n

                def _scsm(q, sc_cT=sc_cT):
                    q.wait_ge(cT.h, sc_cT)
                    q.tensor_tensor(out=sco[:], in0=pTf[:, 0:128], in1=maskin[:],
                                    op=OP.add).then_inc(cV.h, 1); cV.n += 1
                    q.wait_ge(cV.h, cV.n)
                    q.reduce_max(bnd[:, 0:1], sco[:], axis=AX,
                                 negate=True).then_inc(cV.h, 1); cV.n += 1
                    q.wait_ge(cV.h, cV.n)
                    q.tensor_scalar_mul(bnd[:, 1:2], bnd[:, 0:1], 0.0625
                                        ).then_inc(cV.h, 1); cV.n += 1
                VE(_scsm)
                scs_cV = cV.n

                def _sexp2(q, scs_cV=scs_cV):
                    q.wait_ge(cV.h, scs_cV)
                    q.activation(out=ei_s[:], in_=sco[:], func=ACT.Exp,
                                 bias=bnd[:, 1:2], scale=0.0625,
                                 accum_out=bnd[:, 2:3]).then_inc(cS.h, 1); cS.n += 1
                SE(_sexp2)
                se2_cS = cS.n

                def _eitr(q, se2_cS=se2_cS):
                    q.wait_ge(cS.h, se2_cS)
                    q.wait_ge(cV.h, cV.n)
                    q.transpose(pT[:, 0:128], ei_s[:], identb[:]).then_inc(cT.h, 1)
                    cT.n += 1
                TE(_eitr)
                eit_cT = cT.n

                def _eitev(q, eit_cT=eit_cT):
                    q.wait_ge(cT.h, eit_cT)
                    q.reciprocal(bnd[:, 3:4], bnd[:, 2:3]).then_inc(cV.h, 1); cV.n += 1
                    q.tensor_copy(out=eit_s[:], in_=pT[:, 0:128]).then_inc(cV.h, 1)
                    cV.n += 1
                VE(_eitev)
                eit_cV = cV.n

                def _avmm(q, h=h, eit_cV=eit_cV, vtm_cV=vtm_cV):
                    q.wait_ge(cV.h, max(eit_cV, vtm_cV))
                    q.matmul(pTf[:, 128:384], eit_s[:], vtm[:, 256 * h:256 * h + 256],
                             start=True, stop=True).then_inc(cT.h, 1); cT.n += 1
                TE(_avmm)
                av_cT = cT.n

                def _avev(q, h=h, av_cT=av_cT):
                    q.wait_ge(cT.h, av_cT)
                    q.tensor_scalar(out=otm[:, 256 * h:256 * h + 256],
                                    in0=pTf[:, 128:384], scalar1=bnd[:, 3:4],
                                    scalar2=None, op0=OP.mult).then_inc(cV.h, 1)
                    cV.n += 1
                VE(_avev)
            otm_cV = cV.n

            def _otr(q, otm_cV=otm_cV):
                q.wait_ge(cV.h, otm_cV)
                q.wait_ge(cS.h, cS.n)
                for dc in range(8):
                    q.transpose(pT[:, 128 * dc:128 * dc + 128],
                                otm[:, 128 * dc:128 * dc + 128],
                                identb[:]).then_inc(cT.h, 1); cT.n += 1
            TE(_otr)
            otr_cT = cT.n

            def _otrev(q, otr_cT=otr_cT):
                q.wait_ge(cT.h, otr_cT)
                for dc in range(8):
                    q.copy(out=xfm[:, dc, :],
                           in_=pT[:, 128 * dc:128 * dc + 128]).then_inc(cS.h, 1); cS.n += 1
            SE(_otrev)
            ofm_cS = cS.n

            def _opmm(q, ofm_cS=ofm_cS):
                q.wait_ge(cS.h, ofm_cS)
                q.wait_ge(cV.h, cV.n)
                for half, pp in ((0, pC), (1, pD)):
                    for dc in range(8):
                        q.matmul(pp[:, 0:512], xfm[:, dc, :],
                                 opwt_s[:, dc, 512 * half:512 * half + 512],
                                 start=(dc == 0), stop=False)
                    q.matmul(pp[:, 0:512], onesf[32:33, :],
                             brows[32:33, 512 * half:512 * half + 512],
                             start=False, stop=True).then_inc(cT.h, 1); cT.n += 1
            TE(_opmm)
            op_cT = cT.n

            def _y1pre(q, bufi=bufi, op_cT=op_cT, i=i):
                q.wait_ge(cT.h, op_cT)
                if i >= 2:
                    q.wait_ge(sY1[i % 2].h, y1_stored[i - 2])
                q.tensor_tensor(out=y1t[:, bufi, 0:512], in0=x1t[:, bufi, 0:512],
                                in1=pC[:, 0:512], op=OP.add).then_inc(cV.h, 1); cV.n += 1
                q.tensor_tensor(out=y1t[:, bufi, 512:1024], in0=x1t[:, bufi, 512:1024],
                                in1=pD[:, 0:512], op=OP.add).then_inc(cV.h, 1); cV.n += 1
                q.wait_ge(cV.h, cV.n)
                q.bn_stats(stats[:, 0:6], y1t[:, bufi, 0:512]).then_inc(cV.h, 1); cV.n += 1
                q.bn_stats(stats[:, 6:12], y1t[:, bufi, 512:1024]).then_inc(cV.h, 1); cV.n += 1
                q.wait_ge(cV.h, cV.n)
                q.bn_aggr(mv[:], stats[:]).then_inc(cV.h, 1); cV.n += 1
            VE(_y1pre)
            y1pre_cV = cV.n

            def _rstd1(q, y1pre_cV=y1pre_cV):
                q.wait_ge(cV.h, y1pre_cV)
                q.activation(out=rstd[:], in_=mv[:, 1:2], func=ACT.Ln,
                             bias=epsv[:, 0:1], scale=1.0).then_inc(cS.h, 1); cS.n += 1
                q.wait_ge(cS.h, cS.n)
                q.activation(out=rstd[:], in_=rstd[:], func=ACT.Exp,
                             scale=-0.5).then_inc(cS.h, 1); cS.n += 1
            SE(_rstd1)
            rstd1_cS = cS.n

            def _ln1(q, bufi=bufi, rstd1_cS=rstd1_cS):
                q.wait_ge(cS.h, rstd1_cS)
                q.tensor_scalar(out=y1t[:, bufi, :], in0=y1t[:, bufi, :],
                                scalar1=mv[:, 0:1], scalar2=rstd[:, 0:1],
                                op0=OP.subtract, op1=OP.mult).then_inc(cV.h, 1); cV.n += 1
                q.wait_ge(cV.h, cV.n)
                q.tensor_mul(y1t[:, bufi, :], y1t[:, bufi, :], g1r[:]
                             ).then_inc(cV.h, 1); cV.n += 1
                q.wait_ge(cV.h, cV.n)
                q.tensor_add(y1t[:, bufi, :], y1t[:, bufi, :], be1r[:]
                             ).then_inc(cV.h, 1); cV.n += 1
            VE(_ln1)
            y1_cV = cV.n

            def _sty1(q, i=i, bufi=bufi, y1_cV=y1_cV):
                q.wait_ge(cV.h, y1_cV)
                dma(q, sY1[i % 2], y1sp_h[i], y1t[:, bufi, :])
            SY(_sty1)
            y1_stored[i] = sY1[i % 2].n

            for dc in range(8):
                slot = 128 * (dc % 2)

                def _t(q, dc=dc, slot=slot, y1_cV=y1_cV, bufi=bufi):
                    q.wait_ge(cV.h, max(y1_cV, cV.n))
                    q.transpose(pTf[:, slot:slot + 128],
                                y1t[:, bufi, 128 * dc:128 * dc + 128],
                                identf[:]).then_inc(cT.h, 1); cT.n += 1
                TE(_t)
                t_n = cT.n

                def _e(q, slot=slot, t_n=t_n):
                    q.wait_ge(cT.h, t_n)
                    q.tensor_copy(out=fm32[:], in_=pTf[:, slot:slot + 128]
                                  ).then_inc(cV.h, 1); cV.n += 1
                VE(_e)
                e_n = cV.n

                def _lgmm(q, dc=dc, e_n=e_n, i=i):
                    q.wait_ge(cV.h, e_n)
                    q.matmul(pX[:, 0:8], fm32[:], wgc[:, dc, :],
                             start=(dc == 0), stop=(dc == 7)).then_inc(cT.h, 1); cT.n += 1
                TE(_lgmm)
            lg_cT = cT.n

            def _gates(q, lg_cT=lg_cT):
                q.wait_ge(cT.h, lg_cT)
                q.tensor_copy(out=lg[:], in_=pX[:, 0:8]).then_inc(cV.h, 1); cV.n += 1
                q.wait_ge(cV.h, cV.n)
                q.reduce_max(bnd[:, 0:1], lg[:], axis=AX, negate=True
                             ).then_inc(cV.h, 1); cV.n += 1
            VE(_gates)
            gmax_cV = cV.n

            def _gexp(q, gmax_cV=gmax_cV):
                q.wait_ge(cV.h, gmax_cV)
                q.activation(out=eg[:], in_=lg[:], func=ACT.Exp, bias=bnd[:, 0:1],
                             scale=1.0, accum_out=sume[:, 0:1]).then_inc(cS.h, 1)
                cS.n += 1
            SE(_gexp)
            ge_cS = cS.n

            def _gsel(q, i=i, ge_cS=ge_cS):
                q.wait_ge(cS.h, ge_cS)
                q.reciprocal(sume[:], sume[:]).then_inc(cV.h, 1); cV.n += 1
                q.wait_ge(cV.h, cV.n)
                q.tensor_scalar(out=gts[:], in0=eg[:], scalar1=sume[:, 0:1],
                                scalar2=None, op0=OP.mult).then_inc(cV.h, 1); cV.n += 1
                q.wait_ge(cV.h, cV.n)
                q.max(gmx[:], gts[:]).then_inc(cV.h, 1); cV.n += 1
                q.wait_ge(cV.h, cV.n)
                q.max_index(gmi[:], gmx[:], gts[:]).then_inc(cV.h, 1); cV.n += 1
                q.wait_ge(cV.h, cV.n)
                q.tensor_copy(out=gval_all[:, i:i + 1], in_=gmx[:, 0:1])
                q.tensor_copy(out=idxf[:], in_=gmi[:, 0:1]).then_inc(cV.h, 1); cV.n += 1
                q.wait_ge(cV.h, cV.n)
                q.tensor_tensor(out=gts[:], in0=idxf[:].broadcast_to([128, E]),
                                in1=erow[:], op=OP.is_equal).then_inc(cV.h, 1); cV.n += 1
                q.wait_ge(cV.h, cV.n)
                q.tensor_copy(out=onehot_all[:, i, :], in_=gts[:]).then_inc(cV.h, 1)
                cV.n += 1
            VE(_gsel)
            oh_cV = cV.n

            if i > 0:
                def _basev(q, i=i):
                    q.wait_ge(cV.h, cV.n)
                    q.tensor_copy(out=base_all[:, i, :], in_=runc[:]
                                  ).then_inc(cV.h, 1); cV.n += 1
                VE(_basev)
            bv_cV = cV.n

            def _cnt(q, i=i, oh_cV=oh_cV, bv_cV=bv_cV):
                q.wait_ge(cV.h, max(oh_cV, bv_cV))
                q.matmul(pX[:, 8:16], t128c[:], onehot_all[:, i, :], start=True,
                         stop=True)
                q.matmul(pCNT[0:8, 0:8], bselc[:], onehot_all[:, i, :],
                         start=True, stop=True).then_inc(cT.h, 1)
                cT.n += 1
            TE(_cnt)
            cnt_cT = cT.n

            def _pfev(q, i=i, cnt_cT=cnt_cT):
                q.wait_ge(cT.h, cnt_cT)
                q.wait_ge(cV.h, cV.n)
                q.tensor_copy(out=prefix_all[:, i, :], in_=pX[:, 8:16]
                              ).then_inc(cV.h, 1); cV.n += 1
                q.tensor_tensor(out=runc[:], in0=runc[:], in1=pCNT[0:8, 0:8],
                                op=OP.add).then_inc(cV.h, 1); cV.n += 1
            VE(_pfev)

            tile_done_cV[i] = cV.n

        pa_done_cT = cT.n

        def _pcev(q):
            q.wait_ge(cV.h, cV.n)
            q.tensor_copy(out=pc_s[:], in_=runc[:]).then_inc(cV.h, 1); cV.n += 1
        VE(_pcev)
        pc_cV = cV.n

        def _ldb2(q):
            q.wait_ge(cV.h, pc_cV)
            dma(q, sB2G, b2st, b2m_h[:])
        SY(_ldb2)
        b2_n = sB2G.n

        w1t_ld = [0] * 32
        for fc in range(32):
            def _ldw1(q, fc=fc):
                if fc == 0:
                    q.wait_ge(cT.h, pa_done_cT)
                    q.wait_ge(cV.h, pc_cV)
                q.wait_ge(cS.h, cS.n)
                dma(q, sW1, w1raw[:],
                    w1m_h[:, 128 * fc:128 * fc + 128].rearrange("(c p) f -> p c f", p=128))
            GP(_ldw1)
            ld_n = sW1.n

            def _trw1(q, ld_n=ld_n):
                q.wait_ge(sW1.h, ld_n)
                q.wait_ge(cS.h, cS.n)
                for dc in range(8):
                    q.transpose(pT[:, 128 * dc:128 * dc + 128],
                                w1raw[:, 128 * dc:128 * dc + 128],
                                identb[:]).then_inc(cT.h, 1); cT.n += 1
            TE(_trw1)
            tr_n = cT.n

            def _evw1(q, fc=fc, tr_n=tr_n):
                q.wait_ge(cT.h, tr_n)
                for dc in range(8):
                    q.copy(out=w1t_s[:, fc, 128 * dc:128 * dc + 128],
                           in_=pT[:, 128 * dc:128 * dc + 128]).then_inc(cS.h, 1)
                    cS.n += 1
            SE(_evw1)
            w1t_ld[fc] = cS.n

        def _ldw2(q):
            q.wait_ge(cT.h, pa_done_cT)
            for fc in range(32):
                dma(q, sW2, w2f_s[:, fc, :], w2m_h[128 * fc:128 * fc + 128, :])
        GP(_ldw2)
        w2_n = sW2.n

        for half in range(2):
            def _mm(q, half=half):
                q.wait_ge(sW2.h, w2_n)
                q.wait_ge(cS.h, w1t_ld[31])
                q.wait_ge(cV.h, pc_cV if half == 0 else cV.n)
                for fc in range(32):
                    mmi = q.matmul(pX[0:1, 0:512], b1c[:, fc:fc + 1],
                                   w2f_s[:, fc, 512 * half:512 * half + 512],
                                   start=(fc == 0), stop=(fc == 31))
                mmi.then_inc(cT.h, 1); cT.n += 1
            TE(_mm)
            mm_n = cT.n

            def _ev(q, half=half, mm_n=mm_n):
                q.wait_ge(cT.h, mm_n)
                q.wait_ge(sB2G.h, b2_n)
                q.tensor_tensor(out=b12mine[0:1, 512 * half:512 * half + 512],
                                in0=pX[0:1, 0:512],
                                in1=b2st[0:1, 512 * half:512 * half + 512],
                                op=OP.add).then_inc(cV.h, 1); cV.n += 1
            VE(_ev)
        b12_cV = cV.n

        for dc in range(8):
            for half in range(2):
                pp = (pA, pB, pC, pD)[(2 * dc + half) % 4]

                def _fmm(q, dc=dc, half=half, pp=pp):
                    q.wait_ge(cS.h, w1t_ld[31])
                    q.wait_ge(sW2.h, w2_n)
                    q.wait_ge(cV.h, cV.n)
                    for fc in range(32):
                        mmi = q.matmul(pp[:, 0:512], w1t_s[:, fc, 128 * dc:128 * dc + 128],
                                       w2f_s[:, fc, 512 * half:512 * half + 512],
                                       start=(fc == 0), stop=(fc == 31))
                    mmi.then_inc(cT.h, 1); cT.n += 1
                TE(_fmm)
                f_n = cT.n

                def _fev(q, pp=pp, f_n=f_n):
                    q.wait_ge(cT.h, f_n)
                    q.wait_ge(sFst.h, sFst.n)
                    q.tensor_copy(out=w12st, in_=pp[:, 0:512]).then_inc(cV.h, 1)
                    cV.n += 1
                VE(_fev)
                e_n = cV.n

                def _fst(q, dc=dc, half=half, e_n=e_n):
                    q.wait_ge(cV.h, e_n)
                    dma(q, sFst, ccw_in[:, 1024 * dc + 512 * half:1024 * dc + 512 * half + 512],
                        w12st)
                SY(_fst)
        fold_n = sFst.n

        def _ccin(q):
            q.wait_ge(cV.h, max(pc_cV, b12_cV))
            dma(q, sCin, ccf_in[0:1, 0:64].rearrange("a (b e) -> (a b) e", e=8), pc_s[:])
            dma(q, sCin, ccf_in[0:1, 64:64 + D], b12mine)
        SY(_ccin)
        ccin_n = sCin.n

        def _cc(q):
            q.wait_ge(sFst.h, fold_n)
            q.wait_ge(sCin.h, ccin_n)
            q.collective_compute("AllGather", OP.bypass, ins=[ccf_in[:]],
                                 outs=[ccf_out[:]],
                                 replica_groups=RG).then_inc(cCC.h, 1); cCC.n += 1
            q.collective_compute("AllGather", OP.bypass, ins=[ccw_in[:]],
                                 outs=[ccw_out[:]],
                                 replica_groups=RG).then_inc(cCC.h, 1); cCC.n += 1
            q.wait_ge(cCC.h, cCC.n)
            dma(q, sCCl, w12_s.rearrange("p e c n -> p (e c n)"), ccw_out.transpose([1, 0, 2]))
            dma(q, sCCl, b12all, ccf_out[:, 64:64 + D])
            dma(q, sCCl, sco[0:64, 0:8], ccf_out[:, 0:64].rearrange("c (b e) -> c b e", e=8))
        GP(_cc)
        ccout_n = sCCl.n

        def _ld_g2(q):
            q.wait_ge(cV.h, cV.n)
            dma(q, sB2G, g1r[:], g2_h[None, :].partition_broadcast(128))
            dma(q, sB2G, be1r[:], be2_h[None, :].partition_broadcast(128))
        SY(_ld_g2)
        g2_n = sB2G.n

        def _cbase(q):
            q.wait_ge(sCCl.h, ccout_n)
            q.matmul(pX[0:8, 0:8], selcc[:], sco[0:64, 0:8], start=True,
                     stop=True).then_inc(cT.h, 1); cT.n += 1
        TE(_cbase)
        cb_cT = cT.n

        def _cbev(q):
            q.wait_ge(cT.h, cb_cT)
            q.tensor_copy(out=cb_s[:], in_=pX[0:8, 0:8]).then_inc(cV.h, 1); cV.n += 1
        VE(_cbev)
        cb_cV = cV.n

        y1_ld = [0] * NT
        pb_outv = [0] * NT
        pb_st = [0] * NT

        def _pbld0(q):
            q.wait_ge(sY1[0].h, sY1[0].n)
            q.wait_ge(sY1[1].h, sY1[1].n)
            q.wait_ge(sX1[0].h, sX1[0].n)
            q.wait_ge(sX1[1].h, sX1[1].n)
            q.wait_ge(cV.h, cb_cV)
            dma(q, sPB[0], y1t[:, 0, :], y1sp_h[0])
            dma(q, sPB[0], x1t[:, 0, :], x1sp_h[0])
        SY(_pbld0)
        y1_ld[0] = sPB[0].n

        for i in range(NT):
            bufi = i % 2
            if i + 1 < NT:
                def _ld(q, i=i):
                    if i >= 1:
                        q.wait_ge(cV.h, pb_outv[i - 1])
                    dma(q, sPB[(i + 1) % 2], y1t[:, (i + 1) % 2, :], y1sp_h[i + 1])
                    dma(q, sPB[(i + 1) % 2], x1t[:, (i + 1) % 2, :], x1sp_h[i + 1])
                SY(_ld)
                y1_ld[i + 1] = sPB[(i + 1) % 2].n

            def _comb(q, i=i):
                if i == 0:
                    q.wait_ge(sCCl.h, ccout_n)
                    q.wait_ge(cV.h, cb_cV)
                    q.wait_ge(sB2G.h, g2_n)
                q.wait_ge(cT.h, cT.n)
                q.wait_ge(cV.h, cV.n)
                q.tensor_tensor(out=comb8[:], in0=base_all[:, i, :], in1=cb_s[:],
                                op=OP.add).then_inc(cV.h, 1); cV.n += 1
            VE(_comb)
            comb_cV = cV.n

            def _bc(q, comb_cV=comb_cV):
                q.wait_ge(cV.h, comb_cV)
                q.matmul(pX[:, 0:8], bseltc[:], comb8[:], start=True,
                         stop=True).then_inc(cT.h, 1); cT.n += 1
            TE(_bc)
            bc_cT = cT.n

            def _wsel(q, i=i, bc_cT=bc_cT):
                q.wait_ge(cT.h, bc_cT)
                q.tensor_tensor(out=combf[:], in0=prefix_all[:, i, :], in1=pX[:, 0:8],
                                op=OP.add).then_inc(cV.h, 1); cV.n += 1
                q.wait_ge(cV.h, cV.n)
                q.tensor_tensor(out=combf[:], in0=combf[:], in1=onehot_all[:, i, :],
                                op=OP.mult).then_inc(cV.h, 1); cV.n += 1
                q.wait_ge(cV.h, cV.n)
                q.tensor_reduce(out=locs[:], in_=combf[:], axis=AX,
                                op=OP.add).then_inc(cV.h, 1); cV.n += 1
                q.wait_ge(cV.h, cV.n)
                q.tensor_single_scalar(out=keepv[:], in_=locs[:], scalar=float(CAP),
                                       op=OP.is_le).then_inc(cV.h, 1); cV.n += 1
                q.wait_ge(cV.h, cV.n)
                q.tensor_tensor(out=keepv[:], in0=keepv[:], in1=gval_all[:, i:i + 1],
                                op=OP.mult).then_inc(cV.h, 1); cV.n += 1
                q.wait_ge(cV.h, cV.n)
                q.tensor_scalar(out=wall[:], in0=onehot_all[:, i, :],
                                scalar1=keepv[:, 0:1], scalar2=None,
                                op0=OP.mult).then_inc(cV.h, 1); cV.n += 1
                q.memset(macc, 0.0).then_inc(cV.h, 1); cV.n += 1
            VE(_wsel)
            wsel_cV = cV.n

            def _wtr(q, wsel_cV=wsel_cV):
                q.wait_ge(cV.h, wsel_cV)
                q.transpose(pTf[0:8, 0:128], wall[:], identf[:]).then_inc(cT.h, 1)
                cT.n += 1
            TE(_wtr)
            wtr_cT = cT.n

            def _wtrev(q, wtr_cT=wtr_cT):
                q.wait_ge(cT.h, wtr_cT)
                q.tensor_copy(out=wallt[:], in_=pTf[0:8, 0:128]).then_inc(cV.h, 1)
                cV.n += 1
            VE(_wtrev)
            wallt_cV = cV.n

            def _ycast(q, i=i, bufi=bufi, wsel_cV=wsel_cV):
                q.wait_ge(sPB[i % 2].h, y1_ld[i])
                q.wait_ge(cV.h, wsel_cV)
                q.tensor_copy(out=xbf[:], in_=y1t[:, bufi, :]).then_inc(cV.h, 1)
                cV.n += 1
            VE(_ycast)
            yc_cV = cV.n

            def _ytr2(q, yc_cV=yc_cV):
                q.wait_ge(cV.h, yc_cV)
                q.wait_ge(cS.h, cS.n)
                for dc in range(8):
                    q.transpose(pT[:, 128 * dc:128 * dc + 128],
                                xbf[:, 128 * dc:128 * dc + 128],
                                identb[:]).then_inc(cT.h, 1); cT.n += 1
            TE(_ytr2)
            ytr2_cT = cT.n

            def _ytr2ev(q, ytr2_cT=ytr2_cT):
                q.wait_ge(cT.h, ytr2_cT)
                for dc in range(8):
                    q.copy(out=xfm[:, dc, :],
                           in_=pT[:, 128 * dc:128 * dc + 128]).then_inc(cS.h, 1)
                    cS.n += 1
            SE(_ytr2ev)
            yfm_cS = cS.n

            for e_ in range(E):
                p0, p1 = (pA, pB) if e_ % 2 == 0 else (pC, pD)

                def _app(q, e_=e_, p0=p0, p1=p1, yfm_cS=yfm_cS):
                    q.wait_ge(cS.h, yfm_cS)
                    q.wait_ge(cV.h, cV.n)
                    for half, pp in ((0, p0), (1, p1)):
                        for dc in range(8):
                            mmi = q.matmul(pp[:, 0:512], xfm[:, dc, :],
                                           w12_s[:, e_, dc, 512 * half:512 * half + 512],
                                           start=(dc == 0), stop=(dc == 7))
                    mmi.then_inc(cT.h, 1); cT.n += 1
                TE(_app)
                app_cT = cT.n

                def _acc(q, e_=e_, p0=p0, p1=p1, app_cT=app_cT, wsel_cV=wsel_cV):
                    q.wait_ge(cT.h, app_cT)
                    q.wait_ge(cV.h, max(wsel_cV, cV.n))
                    q.scalar_tensor_tensor(out=macc[:, 0:512], in0=p0[:, 0:512],
                                           scalar=wall[:, e_:e_ + 1],
                                           in1=macc[:, 0:512], op0=OP.mult,
                                           op1=OP.add).then_inc(cV.h, 1); cV.n += 1
                    q.scalar_tensor_tensor(out=macc[:, 512:1024], in0=p1[:, 0:512],
                                           scalar=wall[:, e_:e_ + 1],
                                           in1=macc[:, 512:1024], op0=OP.mult,
                                           op1=OP.add).then_inc(cV.h, 1); cV.n += 1
                VE(_acc)
            acc_cV = cV.n

            def _b12t(q, wallt_cV=wallt_cV):
                q.wait_ge(cV.h, wallt_cV)
                q.wait_ge(cV.h, cV.n)
                q.matmul(pX[:, 0:512], wallt[:], b12all[:, 0:512], start=True, stop=True)
                q.matmul(pTf[:, 0:512], wallt[:], b12all[:, 512:1024], start=True,
                         stop=True).then_inc(cT.h, 1); cT.n += 1
            TE(_b12t)
            b12t_cT = cT.n

            def _y2pre(q, bufi=bufi, acc_cV=acc_cV, b12t_cT=b12t_cT, i=i):
                q.wait_ge(cV.h, acc_cV)
                q.wait_ge(cT.h, b12t_cT)
                if i >= 1:
                    q.wait_ge(sOut[(i - 1) % 2].h, pb_st[i - 1])
                q.wait_ge(cV.h, cV.n)
                q.tensor_tensor(out=macc[:, 0:512], in0=macc[:, 0:512],
                                in1=pX[:, 0:512], op=OP.add).then_inc(cV.h, 1); cV.n += 1
                q.tensor_tensor(out=macc[:, 512:1024], in0=macc[:, 512:1024],
                                in1=pTf[:, 0:512], op=OP.add).then_inc(cV.h, 1); cV.n += 1
                q.wait_ge(cV.h, cV.n)
                q.tensor_tensor(out=y2t, in0=y1t[:, bufi, :], in1=macc,
                                op=OP.add).then_inc(cV.h, 1); cV.n += 1
                q.wait_ge(cV.h, cV.n)
                q.bn_stats(stats[:, 0:6], y2t[:, 0:512]).then_inc(cV.h, 1); cV.n += 1
                q.bn_stats(stats[:, 6:12], y2t[:, 512:1024]).then_inc(cV.h, 1); cV.n += 1
                q.wait_ge(cV.h, cV.n)
                q.bn_aggr(mv[:], stats[:]).then_inc(cV.h, 1); cV.n += 1
            VE(_y2pre)
            y2p_cV = cV.n

            def _rstd2(q, y2p_cV=y2p_cV):
                q.wait_ge(cV.h, y2p_cV)
                q.activation(out=rstd[:], in_=mv[:, 1:2], func=ACT.Ln,
                             bias=epsv[:, 0:1], scale=1.0).then_inc(cS.h, 1); cS.n += 1
                q.wait_ge(cS.h, cS.n)
                q.activation(out=rstd[:], in_=rstd[:], func=ACT.Exp,
                             scale=-0.5).then_inc(cS.h, 1); cS.n += 1
            SE(_rstd2)
            rstd2_cS = cS.n

            def _ln2fin(q, bufi=bufi, rstd2_cS=rstd2_cS):
                q.wait_ge(cS.h, rstd2_cS)
                q.tensor_scalar(out=y2t, in0=y2t, scalar1=mv[:, 0:1],
                                scalar2=rstd[:, 0:1], op0=OP.subtract,
                                op1=OP.mult).then_inc(cV.h, 1); cV.n += 1
                q.wait_ge(cV.h, cV.n)
                q.tensor_mul(y2t, y2t, g1r[:]).then_inc(cV.h, 1); cV.n += 1
                q.wait_ge(cV.h, cV.n)
                q.tensor_add(y2t, y2t, be1r[:]).then_inc(cV.h, 1); cV.n += 1
                q.wait_ge(cV.h, cV.n)
                q.tensor_tensor(out=y2t, in0=y2t, in1=x1t[:, bufi, :],
                                op=OP.add).then_inc(cV.h, 1); cV.n += 1
                q.wait_ge(cV.h, cV.n)
                q.bn_stats(stats[:, 0:6], y2t[:, 0:512]).then_inc(cV.h, 1); cV.n += 1
                q.bn_stats(stats[:, 6:12], y2t[:, 512:1024]).then_inc(cV.h, 1); cV.n += 1
                q.wait_ge(cV.h, cV.n)
                q.bn_aggr(mv[:], stats[:]).then_inc(cV.h, 1); cV.n += 1
            VE(_ln2fin)
            x2p_cV = cV.n

            def _rstd3(q, x2p_cV=x2p_cV):
                q.wait_ge(cV.h, x2p_cV)
                q.activation(out=rstd[:], in_=mv[:, 1:2], func=ACT.Ln,
                             bias=epsv[:, 0:1], scale=1.0).then_inc(cS.h, 1); cS.n += 1
                q.wait_ge(cS.h, cS.n)
                q.activation(out=rstd[:], in_=rstd[:], func=ACT.Exp,
                             scale=-0.5).then_inc(cS.h, 1); cS.n += 1
            SE(_rstd3)
            rstd3_cS = cS.n

            def _lnout(q, rstd3_cS=rstd3_cS):
                q.wait_ge(cS.h, rstd3_cS)
                q.tensor_scalar(out=y2t, in0=y2t, scalar1=mv[:, 0:1],
                                scalar2=rstd[:, 0:1], op0=OP.subtract,
                                op1=OP.mult).then_inc(cV.h, 1); cV.n += 1
            VE(_lnout)
            outv_cV = cV.n
            pb_outv[i] = cV.n

            def _stout(q, i=i, outv_cV=outv_cV):
                q.wait_ge(cV.h, outv_cV)
                dma(q, sOut[i % 2], out_h.transpose([1, 0, 2])[16 * i:16 * (i + 1)], y2t)
            SY(_stout)
            pb_st[i] = sOut[i % 2].n

        def _fin(q):
            for s_ in (sCst, sX0[0], sX0[1], sX1[0], sX1[1], sY1[0], sY1[1], sWa,
                       sSt, sW1, sW2, sCCl, sFst, sCin, sPB[0], sPB[1], sOut[0],
                       sOut[1], sB2G):
                q.wait_ge(s_.h, s_.n)
        GP(_fin)

    return nc


def _consts():
    ident_b = np.eye(128, dtype=ml_dtypes.bfloat16)
    ident_f = np.eye(128, dtype=np.float32)
    t1 = np.arange(128)
    n1, b1_ = t1 // 8, t1 % 8
    maskin = np.where((n1[:, None] == n1[None, :]), 0.0, NEG).astype(np.float32)
    t128 = ((b1_[:, None] == b1_[None, :]) & (n1[:, None] <= n1[None, :])
            ).astype(ml_dtypes.bfloat16)
    bsel = (b1_[:, None] == np.arange(8)[None, :]).astype(ml_dtypes.bfloat16)
    bselt = (np.arange(8)[:, None] == b1_[None, :]).astype(np.float32)
    onesf = np.ones((1, 128), np.float32)
    erow = np.broadcast_to(np.arange(8, dtype=np.float32), (128, 8)).copy()
    q16, k16 = np.meshgrid(np.arange(16), np.arange(16), indexing="ij")
    tri = np.where(k16 <= q16, 0.0, NEG).astype(ml_dtypes.bfloat16).reshape(1, 256)
    tri16 = np.broadcast_to(tri, (128, 256)).copy()
    return ident_b, ident_f, maskin, t128, bsel, bselt, onesf, erow, tri16


def _selc(core):
    cc, bb = np.meshgrid(np.arange(8), np.arange(8), indexing="ij")
    flat_c, flat_b = cc.reshape(-1), bb.reshape(-1)
    b = np.arange(8)
    sel = ((flat_b[:, None] < b[None, :]) |
           ((flat_b[:, None] == b[None, :]) & (flat_c[:, None] < core))
           ).astype(np.float32)
    return sel


def kernel(**inputs):
    gi = lambda k: np.ascontiguousarray(np.asarray(inputs[k], dtype=np.float32))
    x = gi("L0_attn_x")
    (ident_b, ident_f, maskin, t128, bsel, bselt, onesf, erow, tri16) = _consts()

    base = {
        "wq": gi("L0_attn_Wq"), "wk": gi("L0_attn_Wk"), "wv": gi("L0_attn_Wv"),
        "wo": gi("L0_attn_Wo"),
        "bq": gi("L0_attn_bq")[None, :], "bk": gi("L0_attn_bk")[None, :],
        "bv": gi("L0_attn_bv")[None, :], "bo": gi("L0_attn_bo")[None, :],
        "ipw": gi("L0_ffn_in_proj_w"), "ipb": gi("L0_ffn_in_proj_b"),
        "opw": gi("L0_ffn_out_proj_w"), "opb": gi("L0_ffn_out_proj_b")[None, :],
        "g1": gi("L0_ffn_ln1_g"), "be1": gi("L0_ffn_ln1_b"),
        "g2": gi("L0_ffn_ln2_g"), "be2": gi("L0_ffn_ln2_b"),
        "wg": gi("L0_ffn_wg"),
        "identb": ident_b, "identf": ident_f, "maskin": maskin,
        "t128": t128, "bsel": bsel, "bselt": bselt, "onesf": onesf, "erow": erow,
        "tri16": tri16,
    }
    w1 = gi("L0_ffn_w1"); b1 = gi("L0_ffn_b1")
    w2 = gi("L0_ffn_w2"); b2 = gi("L0_ffn_b2")

    in_maps = []
    for c in range(NC_):
        m = dict(base)
        m["x"] = np.ascontiguousarray(x[:, NLOC * c:NLOC * (c + 1), :])
        m["w1m"] = w1[c]
        m["w2m"] = w2[c]
        m["b1m"] = b1[c]
        m["b2m"] = np.ascontiguousarray(b2[c][None, :])
        m["selc"] = _selc(c)
        in_maps.append(m)

    nc = build()
    res = run_bass_kernel_spmd(nc, in_maps, core_ids=list(range(NC_)))
    out = np.concatenate([res.results[c]["out"] for c in range(NC_)], axis=1)
    return np.ascontiguousarray(out.astype(np.float32))


if __name__ == "__main__":
    import reference
    ins = reference.setup_inputs()
    ins = {k: np.asarray(v) for k, v in ins.items()}
    out = kernel(**ins)
    ref = np.asarray(reference.reference(**ins))
    err = np.mean((out - ref) ** 2) / np.mean(ref ** 2)
    print("resid var ratio:", err)
